# version 05
# baseline (speedup 1.0000x reference)
import numpy as np
import ml_dtypes
from contextlib import ExitStack
import concourse.bass as bass
import concourse.mybir as mybir
from concourse.bass_utils import run_bass_kernel_spmd

F32 = mybir.dt.float32
BF16 = mybir.dt.bfloat16
AF = mybir.ActivationFunctionType
ALU = mybir.AluOpType
AX = mybir.AxisListType

D = 2048
TF = 4096
TO = 2048
FC = 16
TG = 512
H = 16
HD = 128
E_EXP = 16
I_DIM = 1408
IC = 11
SI = 2816
SIC = 22
TOPK = 6
SCALE = float(1.0 / np.sqrt(128.0))
KTP = [8, 16, 24, 32]
LN_EPS = 1e-5
RMS_EPS = 1e-6
LA_EPS = 1e-6
NEG = -1e9

ENGINES = ("tensor", "vector", "scalar", "gpsimd", "sync")

N_STAGES = 99
SKIP_EARLY = False
DEBUG = False


class Prog:
    def __init__(self):
        self.streams = {e: [] for e in ENGINES}
        self.counts = {}
        self.inflight = {}
        self.n_ops = {e: 0 for e in ENGINES}

    def c(self, sem):
        return self.counts.get(sem, 0)

    def wait(self, eng, sem, val):
        assert val <= self.c(sem), f"wait {sem}>={val} unreachable (cur {self.c(sem)})"
        if val > 0:
            self.streams[eng].append(("w", sem, val))

    CHAIN = {"vector": "dv", "scalar": "ac", "gpsimd": "po"}

    def op(self, eng, fn, inc=None, amt=1):
        ch = self.CHAIN.get(eng)
        if ch is not None and not (inc and inc.startswith(("sa", "la", "lw", "lc", "sy", "bar", "barg"))):
            prior = self.c(ch)
            if prior > 0:
                self.streams[eng].append(("w", ch, prior))
            assert inc is None or inc == ch, f"op on {eng} must inc {ch}, got {inc}"
            inc, amt = ch, 1
        if inc is not None:
            self.counts[inc] = self.c(inc) + amt
        self.streams[eng].append(("o", fn, inc, amt))
        self.n_ops[eng] += 1
        return self.counts.get(inc) if inc else None

    def dma(self, eng, sem, fn, batch_open=False):
        if not batch_open and self.inflight.get(sem, 0) > 0:
            self.wait(eng, sem, self.c(sem))
            self.inflight[sem] = 0
        self.inflight[sem] = self.inflight.get(sem, 0) + 1
        return self.op(eng, fn, inc=sem, amt=16)

    def replay(self, block, sems):
        for eng in ENGINES:
            stream = self.streams[eng]

            def make(stream):
                def body(e):
                    for it in stream:
                        if it[0] == "w":
                            e.wait_ge(sems[it[1]], it[2])
                        else:
                            ins = it[1](e)
                            if it[2] is not None:
                                ins.then_inc(sems[it[2]], it[3])
                return body

            getattr(block, eng)(make(stream))


def build_program():
    nc = bass.Bass("TRN2", target_bir_lowering=False, debug=False)

    def inp(name, shape, dt=F32):
        return nc.dram_tensor(name, shape, dt, kind="ExternalInput").ap()

    x_full = inp("x_full", [D, TF])
    x_own = inp("x_own", [D, TO])
    qwS = inp("qwS", [128, FC * D])
    kwS = inp("kwS", [128, FC * D])
    vwS = inp("vwS", [128, FC * D])
    owS = inp("owS", [128, FC * D])
    wqS = inp("wqS", [128, FC * D])
    wkS = inp("wkS", [128, FC * D])
    wvS = inp("wvS", [128, FC * D])
    woS = inp("woS", [128, FC * D])
    gwS = inp("gwS", [128, FC * E_EXP])
    egS = inp("egS", [E_EXP, IC, 128, FC * 128])
    euS = inp("euS", [E_EXP, IC, 128, FC * 128])
    edT = inp("edT", [E_EXP, I_DIM, D])
    sgS = inp("sgS", [SIC, 128, FC * 128])
    suS = inp("suS", [SIC, 128, FC * 128])
    sdT = inp("sdT", [SI, D])
    qb2 = inp("qb2", [128, FC])
    ob2 = inp("ob2", [128, FC])
    kbr = inp("kbr", [1, D])
    vbr = inp("vbr", [1, D])
    ln1c = inp("ln1c", [128, FC])
    ln2c = inp("ln2c", [128, FC])
    idenb = inp("idenb", [128, 128], BF16)
    mask_tab = inp("mask_tab", [4, 32, 128, TG], BF16)

    def scratch(name, shape, dt=F32):
        kind = "ExternalOutput" if DEBUG else "Internal"
        return nc.dram_tensor(name, shape, dt, kind=kind).ap()

    xnb = scratch("xnb", [D, TF], BF16)
    xnob = scratch("xnob", [D, TO], BF16)
    xnof = scratch("xnof", [D, TO], F32)
    hfm = scratch("hfm", [D, TF], BF16)
    kfm = scratch("kfm", [D, TF], BF16)
    vtm = scratch("vtm", [H, TF, HD], BF16)
    qfm = scratch("qfm", [D, TO], BF16)
    acc1 = scratch("acc1", [D, TO], F32)
    x2d = scratch("x2d", [D, TO], F32)
    cwTdh = scratch("cwTdh", [E_EXP, TO // 2], BF16)
    cwTdl = scratch("cwTdl", [E_EXP, TO // 2], BF16)
    out = nc.dram_tensor("out", [D, TO], F32, kind="ExternalOutput").ap()

    P = Prog()
    SEMS = ["la0", "la1", "lc", "lw0", "lw1", "sa0", "sa1", "sy", "pe", "dv", "ac", "po", "bar", "barg"]

    with ExitStack() as topstack:
        en = topstack.enter_context
        idb = en(nc.sbuf_tensor("idb", [128, 128], BF16))
        ones1 = en(nc.sbuf_tensor("ones1", [1, 128], BF16))
        ones8 = en(nc.sbuf_tensor("ones8", [128, 8], BF16))
        eps_ln = en(nc.sbuf_tensor("eps_ln", [1, 1], F32))
        eps_rms = en(nc.sbuf_tensor("eps_rms", [1, 1], F32))
        qb2s = en(nc.sbuf_tensor("qb2s", [128, FC], F32))
        ob2s = en(nc.sbuf_tensor("ob2s", [128, FC], F32))
        ln1s = en(nc.sbuf_tensor("ln1s", [128, FC], F32))
        ln2s = en(nc.sbuf_tensor("ln2s", [128, FC], F32))
        la_stack = ExitStack()
        kvb = la_stack.enter_context(nc.sbuf_tensor("kvb", [128, H * HD], BF16))
        ksb_h = la_stack.enter_context(nc.sbuf_tensor("ksb_h", [128, 128], BF16))
        ksb_l = la_stack.enter_context(nc.sbuf_tensor("ksb_l", [128, 128], BF16))
        sems = {s: en(nc.semaphore(s)) for s in SEMS}

        P.dma("sync", "lc", lambda e: e.dma_start(out=idb[:], in_=idenb[:, :]))
        P.dma("sync", "lc", lambda e: e.dma_start(out=qb2s[:], in_=qb2[:, :]), batch_open=True)
        P.dma("sync", "lc", lambda e: e.dma_start(out=ob2s[:], in_=ob2[:, :]), batch_open=True)
        P.dma("sync", "lc", lambda e: e.dma_start(out=ln1s[:], in_=ln1c[:, :]), batch_open=True)
        P.dma("sync", "lc", lambda e: e.dma_start(out=ln2s[:], in_=ln2c[:, :]), batch_open=True)
        P.op("vector", lambda e: e.memset(ones1[:], 1.0))
        P.op("vector", lambda e: e.memset(ones8[:], 1.0))
        P.op("vector", lambda e: e.memset(eps_ln[:], LN_EPS))
        const_dv = P.op("vector", lambda e: e.memset(eps_rms[:], RMS_EPS), inc="dv")
        const_la = P.c("lc")
        for eng in ("scalar", "vector", "tensor", "gpsimd"):
            P.wait(eng, "lc", const_la)
        for eng in ("scalar", "tensor", "gpsimd", "sync"):
            P.wait(eng, "dv", const_dv)

        def stage_barrier():
            dma_tot = {s: P.c(s) for s in ("la0", "la1", "lc", "sy",
                                           "lw0", "lw1", "sa0", "sa1")}
            mil = {s: P.c(s) for s in ("pe", "dv", "ac", "po")}
            for eng in ENGINES:
                for s, v in dma_tot.items():
                    P.wait(eng, s, v)
                for s, v in mil.items():
                    P.wait(eng, s, v)
            ack = {}
            ack["pe"] = P.op("tensor", lambda e: e.nop(), inc="pe")
            ack["dv"] = P.op("vector", lambda e: e.nop())
            ack["dv"] = P.c("dv")
            ack["ac"] = P.op("scalar", lambda e: e.nop())
            ack["ac"] = P.c("ac")
            ack["po"] = P.op("gpsimd", lambda e: e.nop())
            ack["po"] = P.c("po")
            for eng in ("sync", "gpsimd"):
                for s in ("pe", "dv", "ac", "po"):
                    P.wait(eng, s, ack[s])

        def bcast_psum(psum_tile, hi, lo, pe_inc=True):
            P.op("tensor", lambda e: e.matmul(psum_tile, ones1[:], hi, start=True, stop=False))
            return P.op("tensor", lambda e: e.matmul(psum_tile, ones1[:], lo, start=False,
                                                     stop=True), inc="pe" if pe_inc else None)

        def stage_ln(pref, src_dram, Tlen, dst_bf, dst_f32):
            with ExitStack() as stk:
                sen = stk.enter_context
                xt = [sen(nc.sbuf_tensor(f"{pref}_xt{i}", [128, FC * TG], F32))
                      for i in range(2)]
                xb = [sen(nc.sbuf_tensor(f"{pref}_xb{i}", [128, FC * TG], BF16))
                      for i in range(2)]
                xq = [sen(nc.sbuf_tensor(f"{pref}_xq{i}", [128, FC * TG], BF16))
                      for i in range(2)]
                xnbuf = sen(nc.sbuf_tensor(f"{pref}_xnb", [128, FC * TG], BF16))
                xnf = sen(nc.sbuf_tensor(f"{pref}_xnf", [128, FC * TG], F32))
                rows = sen(nc.sbuf_tensor(f"{pref}_rows", [1, 6 * TG], F32))
                ab_h = sen(nc.sbuf_tensor(f"{pref}_abh", [1, TG], BF16))
                ab_l = sen(nc.sbuf_tensor(f"{pref}_abl", [1, TG], BF16))
                bb_h = sen(nc.sbuf_tensor(f"{pref}_bbh", [1, TG], BF16))
                bb_l = sen(nc.sbuf_tensor(f"{pref}_bbl", [1, TG], BF16))
                ps_s = sen(nc.psum_tensor(f"{pref}_ps_s", [8, TG], F32))
                ps_q = sen(nc.psum_tensor(f"{pref}_ps_q", [8, TG], F32))
                ps_a = sen(nc.psum_tensor(f"{pref}_ps_a", [128, TG], F32))
                ps_b = sen(nc.psum_tensor(f"{pref}_ps_b", [128, TG], F32))

                ntg = Tlen // TG
                hist = []
                for tg in range(ntg):
                    buf = xt[tg % 2]
                    src = src_dram[:, tg * TG:(tg + 1) * TG].rearrange(
                        "(c p) t -> p c t", p=128)
                    if tg >= 2:
                        P.wait("sync", "ac", hist[tg - 2][0])
                        P.wait("sync", "dv", hist[tg - 2][1])
                    lsem = f"la{tg % 2}"
                    la = P.dma("sync", lsem, lambda e, buf=buf, src=src: e.dma_start(
                        out=buf[:].rearrange("p (c t) -> p c t", c=FC), in_=src))
                    P.wait("scalar", lsem, la)
                    xbb = xb[tg % 2]
                    xqb = xq[tg % 2]
                    if tg >= 2:
                        P.wait("scalar", "pe", hist[tg - 2][3])
                    P.op("scalar", lambda e, buf=buf, xbb=xbb: e.activation(
                        xbb[:], buf[:], AF.Identity))
                    ac1 = P.op("scalar", lambda e, buf=buf, xqb=xqb: e.activation(
                        xqb[:], buf[:], AF.Square), inc="ac")
                    P.wait("tensor", "ac", ac1)
                    if tg >= 1:
                        P.wait("tensor", "dv", hist[tg - 1][4])
                    for c in range(FC):
                        P.op("tensor", lambda e, c=c, xbb=xbb: e.matmul(
                            ps_s[:], ones8[:], xbb[:, c * TG:(c + 1) * TG],
                            start=(c == 0), stop=(c == FC - 1)))
                        P.op("tensor", lambda e, c=c, xqb=xqb: e.matmul(
                            ps_q[:], ones8[:], xqb[:, c * TG:(c + 1) * TG],
                            start=(c == 0), stop=(c == FC - 1)),
                            inc="pe" if c == FC - 1 else None)
                    pe_stats = P.c("pe")
                    mu = rows[:, 0:TG]
                    msq = rows[:, TG:2 * TG]
                    mu2 = rows[:, 2 * TG:3 * TG]
                    var = rows[:, 3 * TG:4 * TG]
                    sd = rows[:, 4 * TG:5 * TG]
                    rs = rows[:, 5 * TG:6 * TG]
                    P.wait("vector", "pe", pe_stats)
                    P.op("vector", lambda e, mu=mu: e.tensor_scalar(
                        mu, ps_s[0:1, :], 1.0 / D, None, ALU.mult))
                    P.op("vector", lambda e, msq=msq: e.tensor_scalar(
                        msq, ps_q[0:1, :], 1.0 / D, None, ALU.mult))
                    P.op("vector", lambda e, mu=mu, mu2=mu2: e.tensor_tensor(
                        mu2, mu, mu, ALU.mult))
                    dv1 = P.op("vector", lambda e, msq=msq, mu2=mu2, var=var: e.tensor_tensor(
                        var, msq, mu2, ALU.subtract), inc="dv")
                    dv_rows = dv1
                    P.wait("scalar", "dv", dv1)
                    ac2 = P.op("scalar", lambda e, var=var, sd=sd: e.activation(
                        sd, var, AF.Sqrt, bias=eps_ln[:]), inc="ac")
                    P.wait("vector", "ac", ac2)
                    dv2 = P.op("vector", lambda e, sd=sd, rs=rs: e.reciprocal(rs, sd),
                               inc="dv")
                    P.wait("vector", "dv", dv2)
                    P.op("vector", lambda e, rs=rs: e.tensor_copy(ab_h[:], rs))
                    P.op("vector", lambda e, rs=rs: e.tensor_tensor(
                        ab_l[:], rs, ab_h[:], ALU.subtract))
                    P.op("vector", lambda e, mu=mu, rs=rs, mu2=mu2: e.scalar_tensor_tensor(
                        mu2, mu, -1.0, rs, ALU.mult, ALU.mult))
                    P.op("vector", lambda e, mu2=mu2: e.tensor_copy(bb_h[:], mu2))
                    dv3 = P.op("vector", lambda e, mu2=mu2: e.tensor_tensor(
                        bb_l[:], mu2, bb_h[:], ALU.subtract), inc="dv")
                    P.wait("tensor", "dv", dv3)
                    bcast_psum(ps_a[:], ab_h[:], ab_l[:], pe_inc=False)
                    pe_b = bcast_psum(ps_b[:], bb_h[:], bb_l[:])
                    P.wait("vector", "pe", pe_b)
                    if tg >= 1:
                        P.wait("vector", hist[tg - 1][2][0], hist[tg - 1][2][1])
                    for c in range(FC):
                        sl = slice(c * TG, (c + 1) * TG)
                        P.op("vector", lambda e, sl=sl, buf=buf: e.tensor_tensor(
                            xnf[:, sl], buf[:, sl], ps_a[:], ALU.mult))
                        if dst_f32 is None:
                            P.op("vector", lambda e, sl=sl: e.tensor_tensor(
                                xnbuf[:, sl], xnf[:, sl], ps_b[:], ALU.add))
                        else:
                            P.op("vector", lambda e, sl=sl: e.tensor_tensor(
                                xnf[:, sl], xnf[:, sl], ps_b[:], ALU.add))
                    dv4 = P.c("dv")
                    ssem = f"sa{tg % 2}"
                    if dst_f32 is not None:
                        P.wait("scalar", "dv", dv4)
                        if tg >= 1:
                            P.wait("scalar", hist[tg - 1][2][0], hist[tg - 1][2][1])
                        ac3 = P.op("scalar", lambda e: e.activation(
                            xnbuf[:], xnf[:], AF.Identity), inc="ac")
                        P.wait("gpsimd", "ac", ac3)
                    else:
                        ac3 = P.c("ac")
                        P.wait("gpsimd", "dv", dv4)
                    dstb = dst_bf[:, tg * TG:(tg + 1) * TG].rearrange(
                        "(c p) t -> p c t", p=128)
                    P.dma("gpsimd", ssem, lambda e, dstb=dstb: e.dma_start(
                        out=dstb, in_=xnbuf[:].rearrange("p (c t) -> p c t", c=FC)))
                    if dst_f32 is not None:
                        dstf = dst_f32[:, tg * TG:(tg + 1) * TG].rearrange(
                            "(c p) t -> p c t", p=128)
                        P.wait("gpsimd", "dv", dv4)
                        P.dma("gpsimd", ssem, lambda e, dstf=dstf: e.dma_start(
                            out=dstf, in_=xnf[:].rearrange("p (c t) -> p c t", c=FC)),
                            batch_open=True)
                    hist.append((ac1, dv4, (ssem, P.c(ssem)), pe_stats, dv_rows))
                stage_barrier()

        if N_STAGES >= 1 and not SKIP_EARLY:
            stage_ln("sA", x_full, TF, xnb, None)
            stage_ln("sB", x_own, TO, xnob, xnof)

        def stage2():
            with ExitStack() as stk:
                sen = stk.enter_context
                kws = sen(nc.sbuf_tensor("s2_kws", [128, FC * D], BF16))
                vws = sen(nc.sbuf_tensor("s2_vws", [128, FC * D], BF16))
                kbb = sen(nc.sbuf_tensor("s2_kbb", [128, D], F32))
                vbb = sen(nc.sbuf_tensor("s2_vbb", [128, D], F32))
                kbrs = sen(nc.sbuf_tensor("s2_kbrs", [1, D], F32))
                vbrs = sen(nc.sbuf_tensor("s2_vbrs", [1, D], F32))
                kb_h = sen(nc.sbuf_tensor("s2_kbh", [1, D], BF16))
                kb_l = sen(nc.sbuf_tensor("s2_kbl", [1, D], BF16))
                vb_h = sen(nc.sbuf_tensor("s2_vbh", [1, D], BF16))
                vb_l = sen(nc.sbuf_tensor("s2_vbl", [1, D], BF16))
                xnt = [sen(nc.sbuf_tensor(f"s2_xnt{i}", [128, FC * 128], BF16))
                       for i in range(2)]
                kpt = sen(nc.sbuf_tensor("s2_kpt", [128, D], BF16))
                vvt = sen(nc.sbuf_tensor("s2_vvt", [128, D], BF16))
                t1 = sen(nc.sbuf_tensor("s2_t1", [128, TG], F32))
                t2 = sen(nc.sbuf_tensor("s2_t2", [128, TG], F32))
                t3 = sen(nc.sbuf_tensor("s2_t3", [128, TG], F32))
                ksf = sen(nc.sbuf_tensor("s2_ksf", [128, 128], F32))
                ps_k = sen(nc.psum_tensor("s2_ps_k", [128, TG], F32))
                ps_v = sen(nc.psum_tensor("s2_ps_v", [128, TG], F32))
                ps_kv = [sen(nc.psum_tensor(f"s2_ps_kv{i}", [128, TG], F32))
                         for i in range(4)]
                ps_ks = sen(nc.psum_tensor("s2_ps_ks", [128, 128], F32))

                kws3 = kws[:].rearrange("p (c e) -> p c e", c=FC)
                vws3 = vws[:].rearrange("p (c e) -> p c e", c=FC)
                P.dma("gpsimd", "lw0", lambda e: e.dma_start(out=kws[:], in_=kwS[:, :]))
                P.dma("gpsimd", "lw0", lambda e: e.dma_start(out=vws[:], in_=vwS[:, :]),
                      batch_open=True)
                lw_w = P.c("lw0")
                P.dma("sync", "lc", lambda e: e.dma_start(out=kbrs[:], in_=kbr[:, :]))
                la_rows = P.dma("sync", "lc", lambda e: e.dma_start(
                    out=vbrs[:], in_=vbr[:, :]), batch_open=True)
                P.wait("vector", "lc", la_rows)
                P.op("vector", lambda e: e.tensor_copy(kb_h[:], kbrs[:]))
                P.op("vector", lambda e: e.tensor_tensor(kb_l[:], kbrs[:], kb_h[:],
                                                         ALU.subtract))
                P.op("vector", lambda e: e.tensor_copy(vb_h[:], vbrs[:]))
                dvr = P.op("vector", lambda e: e.tensor_tensor(
                    vb_l[:], vbrs[:], vb_h[:], ALU.subtract), inc="dv")
                P.wait("tensor", "dv", dvr)
                for egi in range(4):
                    esl = slice(egi * TG, (egi + 1) * TG)
                    if egi > 0:
                        P.wait("tensor", "ac", P.c("ac"))
                    pe1 = bcast_psum(ps_k[:], kb_h[0:1, esl], kb_l[0:1, esl])
                    P.wait("scalar", "pe", pe1)
                    P.op("scalar", lambda e, esl=esl: e.activation(
                        kbb[:, esl], ps_k[:], AF.Identity), inc="ac")
                    pe2 = bcast_psum(ps_v[:], vb_h[0:1, esl], vb_l[0:1, esl])
                    P.wait("scalar", "pe", pe2)
                    P.op("scalar", lambda e, esl=esl: e.activation(
                        vbb[:, esl], ps_v[:], AF.Identity), inc="ac")
                ac_bias = P.c("ac")
                P.wait("tensor", "ac", ac_bias)

                ntc = TF // 128
                for tc in range(ntc):
                    buf = xnt[tc % 2]
                    src = xnb[:, tc * 128:(tc + 1) * 128].rearrange("(c p) t -> p c t", p=128)
                    if tc >= 2:
                        P.wait("sync", "pe", P.c("pe"))
                    lsem = f"la{tc % 2}"
                    la = P.dma("sync", lsem, lambda e, buf=buf, src=src: e.dma_start(
                        out=buf[:].rearrange("p (c t) -> p c t", c=FC), in_=src))
                    P.wait("tensor", lsem, la)
                    if tc == 0:
                        P.wait("tensor", "lw0", lw_w)
                    elu_end = {}
                    for egi in range(4):
                        esl = slice(egi * TG, (egi + 1) * TG)
                        for c in range(FC):
                            lhs = buf[:, c * 128:(c + 1) * 128]
                            P.op("tensor", lambda e, lhs=lhs, c=c, egi=egi: e.matmul(
                                ps_k[:], lhs, kws3[:, c, egi * TG:(egi + 1) * TG],
                                start=(c == 0), stop=(c == FC - 1)))
                            P.op("tensor", lambda e, lhs=lhs, c=c, egi=egi: e.matmul(
                                ps_v[:], lhs, vws3[:, c, egi * TG:(egi + 1) * TG],
                                start=(c == 0), stop=(c == FC - 1)),
                                inc="pe" if c == FC - 1 else None)
                        pe_kv = P.c("pe")
                        P.wait("vector", "pe", pe_kv)
                        P.op("vector", lambda e, esl=esl: e.tensor_tensor(
                            vvt[:, esl], ps_v[:], vbb[:, esl], ALU.add))
                        dv_free = P.op("vector", lambda e, esl=esl: e.tensor_tensor(
                            t1[:], ps_k[:], kbb[:, esl], ALU.add))
                        P.op("vector", lambda e: e.tensor_scalar(
                            t2[:], t1[:], 0.0, None, ALU.min))
                        dv1 = P.op("vector", lambda e: e.tensor_scalar(
                            t3[:], t1[:], 0.0, None, ALU.max), inc="dv")
                        P.wait("scalar", "dv", dv1)
                        ac1 = P.op("scalar", lambda e: e.activation(t2[:], t2[:], AF.Exp),
                                   inc="ac")
                        P.wait("vector", "ac", ac1)
                        dv2 = P.op("vector", lambda e, esl=esl: e.tensor_tensor(
                            kpt[:, esl], t3[:], t2[:], ALU.add), inc="dv")
                        P.wait("tensor", "dv", dv_free)
                        elu_end[esl.start] = dv2
                    P.wait("tensor", "dv", max(elu_end.values()))
                    for h in range(H):
                        hsl = slice(h * HD, (h + 1) * HD)
                        P.op("tensor", lambda e, hsl=hsl, tc=tc, h=h: e.matmul(
                            ps_kv[h // 4][:, (h % 4) * 128:(h % 4) * 128 + 128],
                            kpt[:, hsl], vvt[:, hsl],
                            start=(tc == 0 and h % 4 == 0),
                            stop=(tc == ntc - 1 and h % 4 == 3)))
                        P.op("tensor", lambda e, hsl=hsl, tc=tc, h=h: e.matmul(
                            ps_ks[:, h * 8:h * 8 + 8], kpt[:, hsl], ones8[:],
                            start=(tc == 0 and h == 0),
                            stop=(tc == ntc - 1 and h == H - 1)),
                            inc="pe" if (tc == ntc - 1 and h == H - 1) else None)
                pe_fin = P.c("pe")
                P.wait("scalar", "pe", pe_fin)
                for i in range(4):
                    P.op("scalar", lambda e, i=i: e.activation(
                        kvb[:, i * TG:(i + 1) * TG], ps_kv[i][:], AF.Identity))
                ac_kv = P.op("scalar", lambda e: e.activation(
                    ksf[:], ps_ks[:], AF.Identity), inc="ac")
                P.wait("vector", "ac", ac_kv)
                P.op("vector", lambda e: e.tensor_copy(ksb_h[:], ksf[:]))
                dv_ks = P.op("vector", lambda e: e.tensor_tensor(
                    ksb_l[:], ksf[:], ksb_h[:], ALU.subtract), inc="dv")
                stage_barrier()
                return dv_ks

        dv_ks = stage2() if (N_STAGES >= 2 and not SKIP_EARLY) else 0

        def stage3():
            with ExitStack() as stk:
                sen = stk.enter_context
                qws = sen(nc.sbuf_tensor("s3_qws", [128, FC * D], BF16))
                attn_all = sen(nc.sbuf_tensor("s3_attn", [128, H * 4 * TG], BF16))
                ps_q = [sen(nc.psum_tensor(f"s3_ps_q{i}", [128, TG], F32)) for i in range(2)]
                ps_a = [sen(nc.psum_tensor(f"s3_ps_a{i}", [128, TG], F32)) for i in range(2)]
                ps_z = sen(nc.psum_tensor("s3_ps_z", [8, TG], F32))
                ps_zr = sen(nc.psum_tensor("s3_ps_zr", [128, TG], F32))
                stk3a = ExitStack()
                s3n = stk3a.enter_context
                xnt = [s3n(nc.sbuf_tensor(f"s3_xnt{i}", [128, FC * TG], BF16))
                       for i in range(2)]
                qpt = s3n(nc.sbuf_tensor("s3_qpt", [128, FC * TG], BF16))
                t1 = s3n(nc.sbuf_tensor("s3_t1", [128, TG], F32))
                t2 = s3n(nc.sbuf_tensor("s3_t2", [128, TG], F32))
                t3 = s3n(nc.sbuf_tensor("s3_t3", [128, TG], F32))
                zrow = s3n(nc.sbuf_tensor("s3_zrow", [1, TG], F32))
                zh = s3n(nc.sbuf_tensor("s3_zh", [1, TG], BF16))
                zl = s3n(nc.sbuf_tensor("s3_zl", [1, TG], BF16))
                zrs = s3n(nc.sbuf_tensor("s3_zrs", [128, TG], F32))

                qws3 = qws[:].rearrange("p (c o) -> p c o", c=FC)
                P.dma("gpsimd", "lw0", lambda e: e.dma_start(out=qws[:], in_=qwS[:, :]))
                lw_q = P.c("lw0")

                for tg in range(TO // TG):
                    buf = xnt[tg % 2]
                    src = xnob[:, tg * TG:(tg + 1) * TG].rearrange("(c p) t -> p c t", p=128)
                    if tg >= 2:
                        P.wait("sync", "pe", P.c("pe"))
                    lsem = f"la{tg % 2}"
                    la = P.dma("sync", lsem, lambda e, buf=buf, src=src: e.dma_start(
                        out=buf[:].rearrange("p (c t) -> p c t", c=FC), in_=src))
                    P.wait("tensor", lsem, la)
                    if tg == 0:
                        P.wait("tensor", "lw0", lw_q)
                    consume_dv = {}
                    for oc in range(FC):
                        pq = ps_q[oc % 2]
                        if oc >= 2:
                            P.wait("tensor", "dv", consume_dv[oc - 2])
                        elif tg > 0:
                            P.wait("tensor", "dv", P.c("dv"))
                        for c in range(FC):
                            lhs = qws3[:, c, oc * 128:(oc + 1) * 128]
                            rhs = buf[:, c * TG:(c + 1) * TG]
                            P.op("tensor", lambda e, pq=pq, lhs=lhs, rhs=rhs, c=c: e.matmul(
                                pq[:], lhs, rhs, start=(c == 0), stop=(c == FC - 1)),
                                inc="pe" if c == FC - 1 else None)
                        pe_q = P.c("pe")
                        P.wait("vector", "pe", pe_q)
                        P.op("vector", lambda e, pq=pq, oc=oc: e.tensor_scalar(
                            t1[:], pq[:], qb2s[:, oc:oc + 1], None, ALU.add))
                        consume_dv[oc] = P.c("dv")
                        P.op("vector", lambda e: e.tensor_scalar(
                            t2[:], t1[:], 0.0, None, ALU.min))
                        dv1 = P.op("vector", lambda e: e.tensor_scalar(
                            t3[:], t1[:], 0.0, None, ALU.max), inc="dv")
                        P.wait("scalar", "dv", dv1)
                        ac1 = P.op("scalar", lambda e: e.activation(t2[:], t2[:], AF.Exp),
                                   inc="ac")
                        P.wait("vector", "ac", ac1)
                        dv2 = P.op("vector", lambda e, oc=oc: e.tensor_tensor(
                            qpt[:, oc * TG:(oc + 1) * TG], t3[:], t2[:], ALU.add), inc="dv")
                        P.wait("tensor", "dv", dv2)
                    if tg == 0 and N_STAGES >= 2:
                        P.wait("tensor", "dv", dv_ks)
                    hdva = {}
                    hzr = {}
                    hacz = {}
                    for h in range(H):
                        pa = ps_a[h % 2]
                        qsl = qpt[:, h * TG:(h + 1) * TG]
                        if h >= 2:
                            P.wait("tensor", "dv", hdva[h - 2])
                        elif tg > 0:
                            P.wait("tensor", "dv", P.c("dv"))
                        P.op("tensor", lambda e, pa=pa, h=h, qsl=qsl: e.matmul(
                            pa[:], kvb[:, h * HD:(h + 1) * HD], qsl, start=True, stop=True))
                        if h >= 1:
                            P.wait("tensor", "dv", hzr[h - 1])
                        P.op("tensor", lambda e, h=h, qsl=qsl: e.matmul(
                            ps_z[:], ksb_h[:, h * 8:h * 8 + 8], qsl, start=True, stop=False))
                        pe_a = P.op("tensor", lambda e, h=h, qsl=qsl: e.matmul(
                            ps_z[:], ksb_l[:, h * 8:h * 8 + 8], qsl, start=False, stop=True),
                            inc="pe")
                        P.wait("vector", "pe", pe_a)
                        hzr[h] = P.op("vector", lambda e: e.tensor_scalar(
                            zrow[:], ps_z[0:1, :], LA_EPS, None, ALU.add))
                        dvz = P.op("vector", lambda e: e.reciprocal(zrow[:], zrow[:]),
                                   inc="dv")
                        P.op("vector", lambda e: e.tensor_copy(zh[:], zrow[:]))
                        dvz2 = P.op("vector", lambda e: e.tensor_tensor(
                            zl[:], zrow[:], zh[:], ALU.subtract), inc="dv")
                        P.wait("tensor", "dv", dvz2)
                        if h >= 1:
                            P.wait("tensor", "ac", hacz[h - 1])
                        pe_zr = bcast_psum(ps_zr[:], zh[:, :], zl[:, :])
                        P.wait("scalar", "pe", pe_zr)
                        if h >= 1:
                            P.wait("scalar", "dv", hdva[h - 1])
                        hacz[h] = P.op("scalar", lambda e: e.activation(
                            zrs[:], ps_zr[:], AF.Identity), inc="ac")
                        P.wait("vector", "ac", hacz[h])
                        asl = slice((h * 4 + tg) * TG, (h * 4 + tg) * TG + TG)
                        hdva[h] = P.op("vector", lambda e, pa=pa, asl=asl: e.tensor_tensor(
                            attn_all[:, asl], pa[:], zrs[:], ALU.mult), inc="dv")

                stage_barrier()
                stk3a.close()
                xot = sen(nc.sbuf_tensor("s3_xot", [128, FC * TG], F32))
                a1t = sen(nc.sbuf_tensor("s3_a1t", [128, FC * TG], F32))
                dv_attn = P.c("dv")
                ows3 = qws3
                P.wait("gpsimd", "pe", P.c("pe"))
                lw_o = P.dma("gpsimd", "lw0", lambda e: e.dma_start(
                    out=qws[:], in_=owS[:, :]))
                P.wait("tensor", "dv", dv_attn)
                sa_hist = None
                for tg in range(TO // TG):
                    src = x_own[:, tg * TG:(tg + 1) * TG].rearrange("(c p) t -> p c t", p=128)
                    if sa_hist is not None:
                        P.wait("sync", sa_hist[0], sa_hist[1])
                    lsem = f"la{tg % 2}"
                    la = P.dma("sync", lsem, lambda e, src=src: e.dma_start(
                        out=xot[:].rearrange("p (c t) -> p c t", c=FC), in_=src))
                    if tg == 0:
                        P.wait("tensor", "lw0", lw_o)
                    evac_dv = {}
                    for oc in range(FC):
                        pq = ps_q[oc % 2]
                        if oc >= 2:
                            P.wait("tensor", "dv", evac_dv[oc - 2])
                        elif tg > 0:
                            P.wait("tensor", "dv", P.c("dv"))
                        for ec in range(FC):
                            lhs = ows3[:, ec, oc * 128:(oc + 1) * 128]
                            rhs = attn_all[:, (ec * 4 + tg) * TG:(ec * 4 + tg) * TG + TG]
                            P.op("tensor", lambda e, pq=pq, lhs=lhs, rhs=rhs, ec=ec: e.matmul(
                                pq[:], lhs, rhs, start=(ec == 0), stop=(ec == FC - 1)),
                                inc="pe" if ec == FC - 1 else None)
                        pe_o = P.c("pe")
                        P.wait("vector", "pe", pe_o)
                        if oc == 0:
                            P.wait("vector", lsem, la)
                            if sa_hist is not None:
                                P.wait("vector", sa_hist[0], sa_hist[1])
                        csl = slice(oc * TG, (oc + 1) * TG)
                        P.op("vector", lambda e, pq=pq, csl=csl: e.tensor_tensor(
                            a1t[:, csl], pq[:], xot[:, csl], ALU.add))
                        P.op("vector", lambda e, csl=csl, oc=oc: e.tensor_scalar(
                            a1t[:, csl], a1t[:, csl], ob2s[:, oc:oc + 1], None, ALU.add))
                        evac_dv[oc] = P.c("dv")
                    dv_o = P.c("dv")
                    P.wait("gpsimd", "dv", dv_o)
                    dsta = acc1[:, tg * TG:(tg + 1) * TG].rearrange("(c p) t -> p c t", p=128)
                    ssem = f"sa{tg % 2}"
                    sa_val = P.dma("gpsimd", ssem, lambda e, dsta=dsta: e.dma_start(
                        out=dsta, in_=a1t[:].rearrange("p (c t) -> p c t", c=FC)))
                    sa_hist = (ssem, sa_val)
                stage_barrier()

        if N_STAGES >= 3 and not SKIP_EARLY:
            stage3()
        la_stack.close()

        def stage4k():
            with ExitStack() as stk:
                sen = stk.enter_context
                kws = sen(nc.sbuf_tensor("s4_kws", [128, FC * D], BF16))
                xnt = [sen(nc.sbuf_tensor(f"s4_xnt{i}", [128, FC * TG], BF16))
                       for i in range(2)]
                hb = sen(nc.sbuf_tensor("s4_hb", [128, FC * TG], BF16))
                kout = sen(nc.sbuf_tensor("s4_kout", [128, FC * TG], BF16))
                xqs = [sen(nc.sbuf_tensor(f"s4_xqs{i}", [128, FC * TG], BF16)) for i in range(2)]
                t1 = sen(nc.sbuf_tensor("s4_t1", [128, TG], F32))
                rows = sen(nc.sbuf_tensor("s4_rows", [1, 3 * TG], F32))
                rs_h = sen(nc.sbuf_tensor("s4_rsh", [1, TG], BF16))
                rs_l = sen(nc.sbuf_tensor("s4_rsl", [1, TG], BF16))
                ps_q = sen(nc.psum_tensor("s4_ps_q", [8, TG], F32))
                ps_r = sen(nc.psum_tensor("s4_ps_r", [128, TG], F32))
                ps_k = [sen(nc.psum_tensor(f"s4_ps_k{i}", [128, TG], F32)) for i in range(2)]

                kws3 = kws[:].rearrange("p (c e) -> p c e", c=FC)
                lw_w = P.dma("gpsimd", "lw0", lambda e: e.dma_start(
                    out=kws[:], in_=wkS[:, :]))
                hist = []
                for tg in range(TF // TG):
                    buf = xnt[tg % 2]
                    srcx = xnb[:, tg * TG:(tg + 1) * TG].rearrange("(c p) t -> p c t", p=128)
                    if tg >= 2:
                        P.wait("sync", "ac", hist[tg - 2][0])
                        P.wait("sync", "dv", hist[tg - 2][3])
                    lsem = f"la{tg % 2}"
                    la = P.dma("sync", lsem, lambda e, buf=buf, srcx=srcx: e.dma_start(
                        out=buf[:].rearrange("p (c t) -> p c t", c=FC), in_=srcx))
                    P.wait("scalar", lsem, la)
                    if tg >= 1:
                        P.wait("scalar", "pe", hist[tg - 1][1])
                    xqf = xqs[tg % 2]
                    if tg >= 2:
                        P.wait("scalar", "pe", hist[tg - 2][1])
                    acs = P.op("scalar", lambda e, buf=buf, xqf=xqf: e.activation(
                        xqf[:], buf[:], AF.Square), inc="ac")
                    P.wait("tensor", "ac", acs)
                    for c in range(FC):
                        P.op("tensor", lambda e, c=c, xqf=xqf: e.matmul(
                            ps_q[:], ones8[:], xqf[:, c * TG:(c + 1) * TG],
                            start=(c == 0), stop=(c == FC - 1)),
                            inc="pe" if c == FC - 1 else None)
                    pe_stats = P.c("pe")
                    msq = rows[:, 0:TG]
                    sd = rows[:, TG:2 * TG]
                    rs = rows[:, 2 * TG:3 * TG]
                    P.wait("vector", "pe", pe_stats)
                    P.op("vector", lambda e, msq=msq: e.tensor_scalar(
                        msq, ps_q[0:1, :], 1.0 / D, None, ALU.mult))
                    dv1 = P.c("dv")
                    P.wait("scalar", "dv", dv1)
                    ac2 = P.op("scalar", lambda e, msq=msq, sd=sd: e.activation(
                        sd, msq, AF.Sqrt, bias=eps_rms[:]))
                    P.wait("vector", "ac", ac2)
                    P.op("vector", lambda e, sd=sd, rs=rs: e.reciprocal(rs, sd))
                    P.op("vector", lambda e, rs=rs: e.tensor_copy(rs_h[:], rs))
                    dv2 = P.op("vector", lambda e, rs=rs: e.tensor_tensor(
                        rs_l[:], rs, rs_h[:], ALU.subtract))
                    P.wait("tensor", "dv", dv2)
                    pe_r = bcast_psum(ps_r[:], rs_h[:, :], rs_l[:, :])
                    P.wait("vector", "pe", pe_r)
                    if tg >= 1:
                        P.wait("vector", hist[tg - 1][2][0], hist[tg - 1][2][1])
                    for c in range(FC):
                        sl = slice(c * TG, (c + 1) * TG)
                        P.op("vector", lambda e, sl=sl, buf=buf: e.tensor_tensor(
                            t1[:], buf[:, sl], ps_r[:], ALU.mult))
                        dvh = P.op("vector", lambda e, sl=sl, c=c: e.tensor_scalar(
                            hb[:, sl], t1[:], ln1s[:, c:c + 1], None, ALU.mult))
                    dv_h = P.c("dv")
                    dsth = hfm[:, tg * TG:(tg + 1) * TG].rearrange("(c p) t -> p c t", p=128)
                    P.wait("gpsimd", "dv", dv_h)
                    ssem = f"sa{tg % 2}"
                    P.dma("gpsimd", ssem, lambda e, dsth=dsth: e.dma_start(
                        out=dsth, in_=hb[:].rearrange("p (c t) -> p c t", c=FC)))
                    P.wait("tensor", "dv", dv_h)
                    if tg == 0:
                        P.wait("tensor", "lw0", lw_w)
                    evdv = {}
                    for ec in range(FC):
                        pk = ps_k[ec % 2]
                        if ec >= 2:
                            P.wait("tensor", "ac", evdv[ec - 2])
                        elif tg > 0:
                            P.wait("tensor", "ac", P.c("ac"))
                        for c in range(FC):
                            P.op("tensor", lambda e, pk=pk, c=c, ec=ec: e.matmul(
                                pk[:], kws3[:, c, ec * 128:(ec + 1) * 128],
                                hb[:, c * TG:(c + 1) * TG],
                                start=(c == 0), stop=(c == FC - 1)),
                                inc="pe" if c == FC - 1 else None)
                        pe_k = P.c("pe")
                        P.wait("scalar", "pe", pe_k)
                        if ec < 2 and tg >= 1:
                            P.wait("scalar", hist[tg - 1][2][0], hist[tg - 1][2][1])
                        P.op("scalar", lambda e, pk=pk, ec=ec: e.activation(
                            kout[:, ec * TG:(ec + 1) * TG], pk[:], AF.Identity))
                        evdv[ec] = P.c("ac")
                    ac_k = P.c("ac")
                    dstk = kfm[:, tg * TG:(tg + 1) * TG].rearrange("(c p) t -> p c t", p=128)
                    P.wait("gpsimd", "ac", ac_k)
                    sk = P.dma("gpsimd", ssem, lambda e, dstk=dstk: e.dma_start(
                        out=dstk, in_=kout[:].rearrange("p (c t) -> p c t", c=FC)),
                        batch_open=True)
                    hist.append((P.c("ac"), pe_stats, (ssem, P.c(ssem)), dv_h))
                stage_barrier()

        def stage4v():
            with ExitStack() as stk:
                sen = stk.enter_context
                vws = sen(nc.sbuf_tensor("s4v_vws", [128, FC * D], BF16))
                ht = [sen(nc.sbuf_tensor(f"s4v_ht{i}", [128, FC * 128], BF16))
                      for i in range(2)]
                vout = [sen(nc.sbuf_tensor(f"s4v_vout{i}", [128, D], BF16))
                        for i in range(2)]
                ps_v = [sen(nc.psum_tensor(f"s4v_ps_v{i}", [128, TG], F32)) for i in range(2)]
                vws3 = vws[:].rearrange("p (c e) -> p c e", c=FC)
                lw_w = P.dma("gpsimd", "lw0", lambda e: e.dma_start(
                    out=vws[:], in_=wvS[:, :]))
                hist = []
                vguard = {}
                for tc in range(TF // 128):
                    buf = ht[tc % 2]
                    srch = hfm[:, tc * 128:(tc + 1) * 128].rearrange("(c p) t -> p c t", p=128)
                    if tc >= 2:
                        P.wait("sync", "pe", P.c("pe"))
                    lsem = f"la{tc % 2}"
                    la = P.dma("sync", lsem, lambda e, buf=buf, srch=srch: e.dma_start(
                        out=buf[:].rearrange("p (c t) -> p c t", c=FC), in_=srch))
                    P.wait("tensor", lsem, la)
                    if tc == 0:
                        P.wait("tensor", "lw0", lw_w)
                    vo = vout[tc % 2]
                    for egi in range(4):
                        pv = ps_v[egi % 2]
                        gk = tc * 4 + egi
                        if gk >= 2:
                            P.wait("tensor", "ac", vguard[gk - 2])
                        for c in range(FC):
                            P.op("tensor", lambda e, pv=pv, c=c, egi=egi, buf=buf: e.matmul(
                                pv[:], buf[:, c * 128:(c + 1) * 128],
                                vws3[:, c, egi * TG:(egi + 1) * TG],
                                start=(c == 0), stop=(c == FC - 1)),
                                inc="pe" if c == FC - 1 else None)
                        pe_v = P.c("pe")
                        P.wait("scalar", "pe", pe_v)
                        if tc >= 2 and egi == 0:
                            P.wait("scalar", hist[tc - 2][0][0], hist[tc - 2][0][1])
                        P.op("scalar", lambda e, pv=pv, egi=egi, vo=vo: e.activation(
                            vo[:, egi * TG:(egi + 1) * TG], pv[:], AF.Identity))
                        vguard[gk] = P.c("ac")
                    ac_v = P.c("ac")
                    dstv = vtm[:, tc * 128:(tc + 1) * 128, :].rearrange("h t d -> t h d")
                    P.wait("gpsimd", "ac", ac_v)
                    ssem = f"sa{tc % 2}"
                    P.dma("gpsimd", ssem, lambda e, dstv=dstv, vo=vo: e.dma_start(
                        out=dstv, in_=vo[:].rearrange("p (h d) -> p h d", h=H)))
                    hist.append(((ssem, P.c(ssem)),))
                stage_barrier()

        def stage4q():
            with ExitStack() as stk:
                sen = stk.enter_context
                qws = sen(nc.sbuf_tensor("s4q_qws", [128, FC * D], BF16))
                xnt = [sen(nc.sbuf_tensor(f"s4q_xnt{i}", [128, FC * TG], BF16))
                       for i in range(2)]
                hb = sen(nc.sbuf_tensor("s4q_hb", [128, FC * TG], BF16))
                qout = sen(nc.sbuf_tensor("s4q_qout", [128, FC * TG], BF16))
                xqs = [sen(nc.sbuf_tensor(f"s4q_xqs{i}", [128, FC * TG], BF16)) for i in range(2)]
                t1 = sen(nc.sbuf_tensor("s4q_t1", [128, TG], F32))
                rows = sen(nc.sbuf_tensor("s4q_rows", [1, 3 * TG], F32))
                rs_h = sen(nc.sbuf_tensor("s4q_rsh", [1, TG], BF16))
                rs_l = sen(nc.sbuf_tensor("s4q_rsl", [1, TG], BF16))
                ps_q = sen(nc.psum_tensor("s4q_ps_q", [8, TG], F32))
                ps_r = sen(nc.psum_tensor("s4q_ps_r", [128, TG], F32))
                ps_k = [sen(nc.psum_tensor(f"s4q_ps_k{i}", [128, TG], F32)) for i in range(2)]
                qws3 = qws[:].rearrange("p (c e) -> p c e", c=FC)
                lw_w = P.dma("gpsimd", "lw0", lambda e: e.dma_start(
                    out=qws[:], in_=wqS[:, :]))
                hist = []
                for tg in range(TO // TG):
                    buf = xnt[tg % 2]
                    srcx = xnob[:, tg * TG:(tg + 1) * TG].rearrange("(c p) t -> p c t", p=128)
                    if tg >= 2:
                        P.wait("sync", "ac", hist[tg - 2][0])
                        P.wait("sync", "dv", hist[tg - 2][3])
                    lsem = f"la{tg % 2}"
                    la = P.dma("sync", lsem, lambda e, buf=buf, srcx=srcx: e.dma_start(
                        out=buf[:].rearrange("p (c t) -> p c t", c=FC), in_=srcx))
                    P.wait("scalar", lsem, la)
                    if tg >= 1:
                        P.wait("scalar", "pe", hist[tg - 1][1])
                    xqf = xqs[tg % 2]
                    if tg >= 2:
                        P.wait("scalar", "pe", hist[tg - 2][1])
                    acs = P.op("scalar", lambda e, buf=buf, xqf=xqf: e.activation(
                        xqf[:], buf[:], AF.Square), inc="ac")
                    P.wait("tensor", "ac", acs)
                    for c in range(FC):
                        P.op("tensor", lambda e, c=c, xqf=xqf: e.matmul(
                            ps_q[:], ones8[:], xqf[:, c * TG:(c + 1) * TG],
                            start=(c == 0), stop=(c == FC - 1)),
                            inc="pe" if c == FC - 1 else None)
                    pe_stats = P.c("pe")
                    msq = rows[:, 0:TG]
                    sd = rows[:, TG:2 * TG]
                    rs = rows[:, 2 * TG:3 * TG]
                    P.wait("vector", "pe", pe_stats)
                    P.op("vector", lambda e, msq=msq: e.tensor_scalar(
                        msq, ps_q[0:1, :], 1.0 / D, None, ALU.mult))
                    P.wait("scalar", "dv", P.c("dv"))
                    ac2 = P.op("scalar", lambda e, msq=msq, sd=sd: e.activation(
                        sd, msq, AF.Sqrt, bias=eps_rms[:]))
                    P.wait("vector", "ac", ac2)
                    P.op("vector", lambda e, sd=sd, rs=rs: e.reciprocal(rs, sd))
                    P.op("vector", lambda e, rs=rs: e.tensor_copy(rs_h[:], rs))
                    dv2 = P.op("vector", lambda e, rs=rs: e.tensor_tensor(
                        rs_l[:], rs, rs_h[:], ALU.subtract))
                    P.wait("tensor", "dv", dv2)
                    pe_r = bcast_psum(ps_r[:], rs_h[:, :], rs_l[:, :])
                    P.wait("vector", "pe", pe_r)
                    for c in range(FC):
                        sl = slice(c * TG, (c + 1) * TG)
                        P.op("vector", lambda e, sl=sl, buf=buf: e.tensor_tensor(
                            t1[:], buf[:, sl], ps_r[:], ALU.mult))
                        P.op("vector", lambda e, sl=sl, c=c: e.tensor_scalar(
                            hb[:, sl], t1[:], ln1s[:, c:c + 1], None, ALU.mult))
                    dv_h = P.c("dv")
                    P.wait("tensor", "dv", dv_h)
                    if tg == 0:
                        P.wait("tensor", "lw0", lw_w)
                    evdv = {}
                    for ec in range(FC):
                        pk = ps_k[ec % 2]
                        if ec >= 2:
                            P.wait("tensor", "ac", evdv[ec - 2])
                        elif tg > 0:
                            P.wait("tensor", "ac", P.c("ac"))
                        for c in range(FC):
                            P.op("tensor", lambda e, pk=pk, c=c, ec=ec: e.matmul(
                                pk[:], qws3[:, c, ec * 128:(ec + 1) * 128],
                                hb[:, c * TG:(c + 1) * TG],
                                start=(c == 0), stop=(c == FC - 1)),
                                inc="pe" if c == FC - 1 else None)
                        pe_k = P.c("pe")
                        P.wait("scalar", "pe", pe_k)
                        if ec < 2 and tg >= 1:
                            P.wait("scalar", hist[tg - 1][2][0], hist[tg - 1][2][1])
                        P.op("scalar", lambda e, pk=pk, ec=ec: e.activation(
                            qout[:, ec * TG:(ec + 1) * TG], pk[:], AF.Identity))
                        evdv[ec] = P.c("ac")
                    ac_k = P.c("ac")
                    dstq = qfm[:, tg * TG:(tg + 1) * TG].rearrange("(c p) t -> p c t", p=128)
                    P.wait("gpsimd", "ac", ac_k)
                    ssem = f"sa{tg % 2}"
                    P.dma("gpsimd", ssem, lambda e, dstq=dstq: e.dma_start(
                        out=dstq, in_=qout[:].rearrange("p (c t) -> p c t", c=FC)))
                    hist.append((P.c("ac"), pe_stats, (ssem, P.c(ssem)), dv_h))
                stage_barrier()

        if N_STAGES >= 4 and not SKIP_EARLY:
            stage4k()
            stage4v()
            stage4q()

        def stage5():
            with ExitStack() as stk:
                sen = stk.enter_context
                wos = sen(nc.sbuf_tensor("s5_wos", [128, FC * D], BF16))
                maskb = sen(nc.sbuf_tensor("s5_maskb", [128, 32 * TG], BF16))
                aob = sen(nc.sbuf_tensor("s5_aob", [128, H * TG], BF16))
                kb = [sen(nc.sbuf_tensor(f"s5_kb{i}", [128, 32 * 128], BF16))
                      for i in range(2)]
                vb = [sen(nc.sbuf_tensor(f"s5_vb{i}", [128, 32 * 128], BF16))
                      for i in range(2)]
                qb = [sen(nc.sbuf_tensor(f"s5_qb{i}", [128, TG], BF16)) for i in range(2)]
                q2b = sen(nc.sbuf_tensor("s5_q2b", [128, TG], BF16))
                sm = [sen(nc.sbuf_tensor(f"s5_sm{i}", [128, TG], F32)) for i in range(2)]
                pt = [sen(nc.sbuf_tensor(f"s5_pt{i}", [128, TG], BF16)) for i in range(2)]
                zrs = sen(nc.sbuf_tensor("s5_zrs", [128, TG], F32))
                zrow = sen(nc.sbuf_tensor("s5_zrow", [1, TG], F32))
                zh = sen(nc.sbuf_tensor("s5_zh", [1, TG], BF16))
                zl = sen(nc.sbuf_tensor("s5_zl", [1, TG], BF16))
                xot = [sen(nc.sbuf_tensor(f"s5_xot{i}", [128, TG], F32)) for i in range(2)]
                x2t = [sen(nc.sbuf_tensor(f"s5_x2t{i}", [128, TG], F32)) for i in range(2)]
                mkrow = sen(nc.sbuf_tensor("s5_mkrow", [1, FC], F32))
                crow = sen(nc.sbuf_tensor("s5_crow", [1, 8], F32))
                c1b = sen(nc.sbuf_tensor("s5_c1b", [1, 1], BF16))
                biasvec = sen(nc.sbuf_tensor("s5_biasvec", [128, 1], F32))
                ps_S = [sen(nc.psum_tensor(f"s5_ps_S{i}", [128, TG], F32)) for i in range(2)]
                ps_O = [sen(nc.psum_tensor(f"s5_ps_O{i}", [128, TG], F32)) for i in range(2)]
                ps_z = [sen(nc.psum_tensor(f"s5_ps_z{i}", [8, TG], F32)) for i in range(2)]
                ps_nc = sen(nc.psum_tensor("s5_ps_nc", [128, TG], F32))
                ps_zr = sen(nc.psum_tensor("s5_ps_zr", [128, TG], F32))

                wos3 = wos[:].rearrange("p (c e) -> p c e", c=FC)
                lw_w = P.dma("gpsimd", "lw0", lambda e: e.dma_start(
                    out=wos[:], in_=woS[:, :]))

                q2p = [q2b, zrs]
                q2x = [sen(nc.sbuf_tensor(f"s5_q2x{i}", [128, TG], BF16)) for i in range(2)]
                c16 = sen(nc.sbuf_tensor("s5_c16", [1, E_EXP], F32))
                c16b = sen(nc.sbuf_tensor("s5_c16b", [1, E_EXP], BF16))
                biasA = sen(nc.sbuf_tensor("s5_biasA", [128, FC], F32))
                sqe = {}
                mme = {}
                rede = {}
                for h in range(H):
                    buf = kb[h % 2]
                    qv = vb[h % 2]
                    srck = kfm[h * 128:(h + 1) * 128, :]
                    srcq = qfm[h * 128:(h + 1) * 128, :]
                    lsem = f"la{h % 2}"
                    if h >= 2:
                        P.wait("sync", "ac", sqe[h - 2])
                    la = P.dma("sync", lsem, lambda e, buf=buf, srck=srck: e.dma_start(
                        out=buf[:], in_=srck))
                    la = P.dma("sync", lsem, lambda e, qv=qv, srcq=srcq: e.dma_start(
                        out=qv[:, 0:TO], in_=srcq), batch_open=True)
                    P.wait("scalar", lsem, la)
                    for g in range(12):
                        tile = (buf[:, g * TG:(g + 1) * TG] if g < 8
                                else qv[:, (g - 8) * TG:(g - 7) * TG])
                        q2t = q2x[g % 2]
                        gp = (h, g - 2) if g >= 2 else (h - 1, 10 + g)
                        if gp in mme:
                            P.wait("scalar", "pe", mme[gp])
                        acs = P.op("scalar", lambda e, tile=tile, q2t=q2t: e.activation(
                            q2t[:], tile, AF.Square))
                        P.wait("tensor", "ac", acs)
                        if gp in rede:
                            P.wait("tensor", "dv", rede[gp])
                        pS = ps_S[g % 2]
                        P.op("tensor", lambda e, pS=pS, q2t=q2t: e.matmul(
                            pS[0:8, :], ones8[:], q2t[:], start=True, stop=True),
                            inc="pe")
                        mme[(h, g)] = P.c("pe")
                        P.wait("vector", "pe", mme[(h, g)])
                        dst = crow[:, 0:1] if g < 8 else crow[:, 1:2]
                        if g in (0, 8):
                            P.op("vector", lambda e, pS=pS, dst=dst: e.tensor_reduce(
                                dst, pS[0:1, :], AX.X, ALU.max))
                        else:
                            P.op("vector", lambda e, pS=pS: e.tensor_reduce(
                                crow[:, 2:3], pS[0:1, :], AX.X, ALU.max))
                            P.op("vector", lambda e, dst=dst: e.tensor_tensor(
                                dst, dst, crow[:, 2:3], ALU.max))
                        rede[(h, g)] = P.c("dv")
                        sqe[h] = P.c("ac")
                    dvm = P.op("vector", lambda e: e.tensor_tensor(
                        crow[:, 3:4], crow[:, 0:1], crow[:, 1:2], ALU.mult))
                    P.wait("scalar", "dv", dvm)
                    P.op("scalar", lambda e, h=h: e.activation(
                        c16[:, h:h + 1], crow[:, 3:4], AF.Sqrt))
                    sqe[h] = P.c("ac")
                P.wait("vector", "ac", P.c("ac"))
                dvc = P.op("vector", lambda e: e.tensor_copy(c16b[:], c16[:]))
                P.wait("tensor", "dv", dvc)
                peb = P.op("tensor", lambda e: e.matmul(
                    ps_nc[:, 0:E_EXP], ones1[:], c16b[:], start=True, stop=True),
                    inc="pe")
                P.wait("vector", "pe", peb)
                dvb_all = P.op("vector", lambda e: e.tensor_scalar(
                    biasA[:], ps_nc[:, 0:E_EXP], -SCALE, None, ALU.mult))
                P.wait("scalar", "dv", dvb_all)
                P.wait("tensor", "dv", dvb_all)
                mk_dv = P.c("dv")

                evac = {}
                for j in range(4):
                    ktp = KTP[j]
                    srcm = mask_tab[j, 0:ktp].rearrange("k p t -> p k t")
                    if j > 0:
                        P.wait("sync", "dv", P.c("dv"))
                    lam = P.dma("sync", "lc", lambda e, srcm=srcm, ktp=ktp: e.dma_start(
                        out=maskb[:].rearrange("p (k t) -> p k t", k=32)[:, 0:ktp, :],
                        in_=srcm))
                    h_fin = {}
                    h_dva = {}
                    h_zread = {}
                    for h in range(H):
                        kbuf = kb[h % 2]
                        vbuf = vb[h % 2]
                        qbuf = qb[h % 2]
                        lsem = f"la{h % 2}"
                        if h >= 2:
                            P.wait("sync", "pe", h_fin[h - 2])
                        else:
                            P.wait("sync", "pe", P.c("pe"))
                            P.wait("sync", "ac", P.c("ac"))
                        la = P.dma("sync", lsem, lambda e, kbuf=kbuf, h=h, ktp=ktp:
                                   e.dma_start(out=kbuf[:, 0:ktp * 128],
                                               in_=kfm[h * 128:(h + 1) * 128, 0:ktp * 128]))
                        srcv = vtm[h, 0:ktp * 128, :].rearrange("(kt p) d -> p kt d", p=128)
                        P.dma("sync", lsem, lambda e, vbuf=vbuf, srcv=srcv, ktp=ktp:
                              e.dma_start(out=vbuf[:].rearrange(
                                  "p (kt d) -> p kt d", kt=32)[:, 0:ktp, :], in_=srcv),
                              batch_open=True)
                        srcq = qfm[h * 128:(h + 1) * 128, j * TG:(j + 1) * TG]
                        la = P.dma("sync", lsem, lambda e, qbuf=qbuf, srcq=srcq:
                                   e.dma_start(out=qbuf[:], in_=srcq), batch_open=True)
                        P.wait("scalar", lsem, la)
                        P.wait("tensor", lsem, la)
                        if j == 0 and h == 0:
                            P.wait("tensor", "lc", lam)
                            P.wait("vector", "lc", lam)
                        elif h == 0:
                            P.wait("vector", "lc", lam)
                        guards = {}
                        if h >= 2:
                            P.wait("tensor", "dv", max(h_dva[h - 2], h_zread[h - 2]))
                        elif j > 0:
                            P.wait("tensor", "dv", P.c("dv"))
                        for kt in range(ktp):
                            pS = ps_S[kt % 2]
                            masked = kt >= ktp - 8
                            if kt >= 2:
                                g0 = guards[kt - 2]
                                if g0[0] is not None:
                                    P.wait("tensor", "dv", g0[0])
                                else:
                                    P.wait("tensor", "ac", g0[1])
                            P.op("tensor", lambda e, pS=pS, kbuf=kbuf, qbuf=qbuf, kt=kt:
                                 e.matmul(pS[:], kbuf[:, kt * 128:(kt + 1) * 128],
                                          qbuf[:], start=True, stop=True), inc="pe")
                            peS = P.c("pe")
                            if masked:
                                P.wait("vector", "pe", peS)
                                if kt >= 2:
                                    P.wait("vector", "ac", guards[kt - 2][1])
                                dvS = P.op("vector", lambda e, pS=pS, kt=kt,
                                           smb=sm[kt % 2]: e.tensor_tensor(
                                               smb[:], pS[:],
                                               maskb[:, kt * TG:(kt + 1) * TG],
                                               ALU.add))
                                P.wait("scalar", "dv", dvS)
                                if kt >= 2:
                                    P.wait("scalar", "pe", guards[kt - 2][2])
                                acE = P.op("scalar", lambda e, kt=kt, smb=sm[kt % 2],
                                           ptb=pt[kt % 2], h=h: e.activation(
                                               ptb[:], smb[:], AF.Exp,
                                               bias=biasA[:, h:h + 1], scale=SCALE))
                            else:
                                dvS = None
                                P.wait("scalar", "pe", peS)
                                if kt >= 2:
                                    P.wait("scalar", "pe", guards[kt - 2][2])
                                acE = P.op("scalar", lambda e, pS=pS, kt=kt,
                                           ptb=pt[kt % 2], h=h: e.activation(
                                               ptb[:], pS[:], AF.Exp,
                                               bias=biasA[:, h:h + 1], scale=SCALE))
                            P.wait("tensor", "ac", acE)
                            P.op("tensor", lambda e, vbuf=vbuf, kt=kt, ptb=pt[kt % 2],
                                 pO=ps_O[h % 2], ktp=ktp: e.matmul(
                                     pO[:], vbuf[:, kt * 128:(kt + 1) * 128], ptb[:],
                                     start=(kt == 0), stop=(kt == ktp - 1)))
                            peAV = P.op("tensor", lambda e, kt=kt, ptb=pt[kt % 2], ktp=ktp,
                                        pz=ps_z[h % 2]: e.matmul(
                                            pz[:], ones8[:], ptb[:],
                                            start=(kt == 0), stop=(kt == ktp - 1)),
                                        inc="pe")
                            guards[kt] = (dvS, acE, peAV)
                        pe_fin = P.c("pe")
                        h_fin[h] = pe_fin
                        P.wait("vector", "pe", pe_fin)
                        zr0 = P.op("vector", lambda e, pz=ps_z[h % 2]: e.tensor_copy(
                            zrow[:], pz[0:1, :]))
                        h_zread[h] = zr0
                        P.op("vector", lambda e: e.reciprocal(zrow[:], zrow[:]))
                        P.op("vector", lambda e: e.tensor_copy(zh[:], zrow[:]))
                        dvz = P.op("vector", lambda e: e.tensor_tensor(
                            zl[:], zrow[:], zh[:], ALU.subtract))
                        P.wait("tensor", "dv", dvz)
                        pe_zr = bcast_psum(ps_zr[:], zh[:, :], zl[:, :])
                        P.wait("scalar", "pe", pe_zr)
                        ac_zr = P.op("scalar", lambda e: e.activation(
                            zrs[:], ps_zr[:], AF.Identity))
                        P.wait("vector", "ac", ac_zr)
                        dva = P.op("vector", lambda e, h=h, pO=ps_O[h % 2]: e.tensor_tensor(
                            aob[:, h * TG:(h + 1) * TG], pO[:], zrs[:], ALU.mult))
                        h_dva[h] = dva
                    dv_ao = max(h_dva.values())
                    P.wait("tensor", "dv", dv_ao)
                    if j == 0:
                        P.wait("tensor", "lw0", lw_w)
                    for oc in range(FC):
                        gi = j * FC + oc
                        pw = ps_S[oc % 2]
                        xob = xot[oc % 2]
                        lsem = f"la{oc % 2}"
                        srcx = xnof[oc * 128:(oc + 1) * 128, j * TG:(j + 1) * TG]
                        if gi >= 2:
                            P.wait("sync", "dv", evac[gi - 2][0])
                        la = P.dma("sync", lsem, lambda e, xob=xob, srcx=srcx:
                                   e.dma_start(out=xob[:], in_=srcx))
                        if gi >= 2:
                            P.wait("tensor", "dv", evac[gi - 2][0])
                        for ec in range(FC):
                            P.op("tensor", lambda e, pw=pw, ec=ec, oc=oc: e.matmul(
                                pw[:], wos3[:, ec, oc * 128:(oc + 1) * 128],
                                aob[:, ec * TG:(ec + 1) * TG],
                                start=(ec == 0), stop=(ec == FC - 1)),
                                inc="pe" if ec == FC - 1 else None)
                        pe_w = P.c("pe")
                        P.wait("vector", "pe", pe_w)
                        P.wait("vector", lsem, la)
                        if gi >= 2:
                            P.wait("vector", evac[gi - 2][1][0], evac[gi - 2][1][1])
                        x2b = x2t[oc % 2]
                        dvw = P.op("vector", lambda e, pw=pw, xob=xob, x2b=x2b:
                                   e.tensor_tensor(x2b[:], pw[:], xob[:], ALU.add))
                        dstx = x2d[oc * 128:(oc + 1) * 128, j * TG:(j + 1) * TG]
                        P.wait("gpsimd", "dv", dvw)
                        ssem = f"sa{oc % 2}"
                        sv = P.dma("gpsimd", ssem, lambda e, dstx=dstx, x2b=x2b:
                                   e.dma_start(out=dstx, in_=x2b[:]))
                        evac[gi] = (dvw, (ssem, sv))
                stage_barrier()

        if N_STAGES >= 5 and not SKIP_EARLY:
            stage5()

        def stage7(p):
            TP = 1024
            with ExitStack() as stk:
                sen = stk.enter_context
                tfm = sen(nc.sbuf_tensor(f"s7_tfm{p}", [128, FC * TP], BF16))
                accb = sen(nc.sbuf_tensor(f"s7_acc{p}", [128, FC * TP], F32))
                edres = sen(nc.sbuf_tensor(f"s7_ed{p}", [128, IC * D], BF16))
                egb = [sen(nc.sbuf_tensor(f"s7_egb{p}_{i}", [128, FC * 128], BF16))
                       for i in range(2)]
                eub = [sen(nc.sbuf_tensor(f"s7_eub{p}_{i}", [128, FC * 128], BF16))
                       for i in range(2)]
                hh = [sen(nc.sbuf_tensor(f"s7_hh{p}_{i}", [128, IC * TG], BF16))
                      for i in range(2)]
                sgt = sen(nc.sbuf_tensor(f"s7_sgt{p}", [128, TG], F32))
                h1t = sen(nc.sbuf_tensor(f"s7_h1t{p}", [128, TG], F32))
                cwh_r = sen(nc.sbuf_tensor(f"s7_cwhr{p}", [1, TP], BF16))
                cwl_r = sen(nc.sbuf_tensor(f"s7_cwlr{p}", [1, TP], BF16))

                col0 = p * TP

                with ExitStack() as stkA:
                    ps_stats = stkA.enter_context(
                        nc.psum_tensor(f"s7_ps_st{p}", [8, TG], F32))
                    ps_r = stkA.enter_context(
                        nc.psum_tensor(f"s7_ps_r{p}", [128, TG], F32))
                    senA = stkA.enter_context
                    x2c = [senA(nc.sbuf_tensor(f"s7_x2c{p}_{i}", [128, TG], F32))
                           for i in range(2)]
                    a1c = [senA(nc.sbuf_tensor(f"s7_a1c{p}_{i}", [128, TG], F32))
                           for i in range(2)]
                    xq = senA(nc.sbuf_tensor(f"s7_xq{p}", [128, 2 * TG], BF16))
                    rows = senA(nc.sbuf_tensor(f"s7_rows{p}", [1, 2 * TG], F32))
                    rs_h = senA(nc.sbuf_tensor(f"s7_rsh{p}", [1, TG], BF16))
                    rs_l = senA(nc.sbuf_tensor(f"s7_rsl{p}", [1, TG], BF16))
                    for tgo in range(2):
                        co = col0 + tgo * TG
                        P.wait("sync", "ac", P.c("ac"))
                        P.wait("sync", "dv", P.c("dv"))
                        mmpe = {}
                        for c in range(FC):
                            xc = x2c[c % 2]
                            lsem = f"la{c % 2}"
                            if c >= 2:
                                P.wait("sync", "ac", mmpe[c - 2][1])
                            la = P.dma("sync", lsem, lambda e, xc=xc, c=c, co=co:
                                       e.dma_start(out=xc[:], in_=x2d[
                                           c * 128:(c + 1) * 128, co:co + TG]))
                            xq2 = xq[:, (c % 2) * TG:(c % 2) * TG + TG]
                            P.wait("scalar", lsem, la)
                            if c >= 2:
                                P.wait("scalar", "pe", mmpe[c - 2][0])
                            acs = P.op("scalar", lambda e, xc=xc, xq2=xq2: e.activation(
                                xq2, xc[:], AF.Square))
                            P.wait("tensor", "ac", acs)
                            P.op("tensor", lambda e, c=c, xq2=xq2: e.matmul(
                                ps_stats[:], ones8[:], xq2,
                                start=(c == 0), stop=(c == FC - 1)), inc="pe")
                            mmpe[c] = (P.c("pe"), acs)
                        pe_stats = P.c("pe")
                        msq = rows[:, 0:TG]
                        sd = rows[:, 0:TG]
                        rs = rows[:, TG:2 * TG]
                        P.wait("vector", "pe", pe_stats)
                        P.op("vector", lambda e, msq=msq: e.tensor_scalar(
                            msq, ps_stats[0:1, :], 1.0 / D, None, ALU.mult))
                        P.wait("scalar", "dv", P.c("dv"))
                        ac2 = P.op("scalar", lambda e, msq=msq, sd=sd: e.activation(
                            sd, msq, AF.Sqrt, bias=eps_rms[:]))
                        P.wait("vector", "ac", ac2)
                        P.op("vector", lambda e, sd=sd, rs=rs: e.reciprocal(rs, sd))
                        P.op("vector", lambda e, rs=rs: e.tensor_copy(rs_h[:], rs))
                        dv2 = P.op("vector", lambda e, rs=rs: e.tensor_tensor(
                            rs_l[:], rs, rs_h[:], ALU.subtract))
                        P.wait("tensor", "dv", dv2)
                        pe_r = bcast_psum(ps_r[:], rs_h[:, :], rs_l[:, :])
                        P.wait("vector", "pe", pe_r)
                        P.wait("sync", "ac", P.c("ac"))
                        P.wait("sync", "dv", P.c("dv"))
                        prev = {}
                        for c in range(FC):
                            xc = x2c[c % 2]
                            ac1t = a1c[c % 2]
                            lsem = f"la{c % 2}"
                            if c >= 2:
                                P.wait("sync", "dv", prev[c - 2])
                            la = P.dma("sync", lsem, lambda e, xc=xc, c=c, co=co:
                                       e.dma_start(out=xc[:], in_=x2d[
                                           c * 128:(c + 1) * 128, co:co + TG]))
                            la = P.dma("sync", lsem, lambda e, ac1t=ac1t, c=c, co=co:
                                       e.dma_start(out=ac1t[:], in_=acc1[
                                           c * 128:(c + 1) * 128, co:co + TG]),
                                       batch_open=True)
                            tsl = slice(c * TP + tgo * TG, c * TP + tgo * TG + TG)
                            P.wait("vector", lsem, la)
                            P.op("vector", lambda e, xc=xc, tsl=tsl: e.tensor_tensor(
                                h1t[:], xc[:], ps_r[:], ALU.mult))
                            P.op("vector", lambda e, tsl=tsl, c=c: e.tensor_scalar(
                                tfm[:, tsl], h1t[:], ln2s[:, c:c + 1], None, ALU.mult))
                            dvp = P.op("vector", lambda e, xc=xc, ac1t=ac1t, tsl=tsl:
                                       e.tensor_tensor(accb[:, tsl], xc[:], ac1t[:],
                                                       ALU.add))
                            prev[c] = dvp
                    stage_barrier()

                with ExitStack() as stkB:
                    ps_g = stkB.enter_context(
                        nc.psum_tensor(f"s7_ps_g{p}", [128, E_EXP], F32))
                    ps_cwT = stkB.enter_context(
                        nc.psum_tensor(f"s7_ps_cwT{p}", [16, 128], F32))
                    senB = stkB.enter_context
                    gws = senB(nc.sbuf_tensor(f"s7_gws{p}", [128, FC * E_EXP], BF16))
                    lg = senB(nc.sbuf_tensor(f"s7_lg{p}", [128, E_EXP], F32))
                    lgw = senB(nc.sbuf_tensor(f"s7_lgw{p}", [128, E_EXP], F32))
                    sel = senB(nc.sbuf_tensor(f"s7_sel{p}", [128, E_EXP], F32))
                    ex = senB(nc.sbuf_tensor(f"s7_ex{p}", [128, E_EXP], F32))
                    mcol = senB(nc.sbuf_tensor(f"s7_m{p}", [128, 8], F32))
                    negm0 = senB(nc.sbuf_tensor(f"s7_negm0{p}", [128, 1], F32))
                    ssum = senB(nc.sbuf_tensor(f"s7_ssum{p}", [128, 2], F32))
                    cwh = senB(nc.sbuf_tensor(f"s7_cwh{p}", [128, E_EXP], BF16))
                    cwl = senB(nc.sbuf_tensor(f"s7_cwl{p}", [128, E_EXP], BF16))
                    tcwh = senB(nc.sbuf_tensor(f"s7_tcwh{p}", [16, 128], BF16))
                    tcwl = senB(nc.sbuf_tensor(f"s7_tcwl{p}", [16, 128], BF16))
                    lag = P.dma("gpsimd", "lw0", lambda e: e.dma_start(
                        out=gws[:], in_=gwS[:, :]))
                    P.wait("tensor", "lw0", lag)
                    for ti in range(8):
                        if ti > 0:
                            P.wait("tensor", "dv", P.c("dv"))
                        for c in range(FC):
                            P.op("tensor", lambda e, c=c, ti=ti: e.matmul(
                                ps_g[:], tfm[:, c * TP + ti * 128:c * TP + ti * 128 + 128],
                                gws[:].rearrange("p (c e) -> p c e", c=FC)[:, c, :],
                                start=(c == 0), stop=(c == FC - 1)),
                                inc="pe" if c == FC - 1 else None)
                        pe_g = P.c("pe")
                        P.wait("vector", "pe", pe_g)
                        P.op("vector", lambda e: e.tensor_copy(lg[:], ps_g[:]))
                        P.op("vector", lambda e: e.tensor_copy(lgw[:], lg[:]))
                        for r_ in range(TOPK):
                            P.op("vector", lambda e, r_=r_: e.tensor_reduce(
                                mcol[:, r_:r_ + 1], lgw[:], AX.X, ALU.max))
                            P.op("vector", lambda e, r_=r_: e.tensor_scalar(
                                sel[:], lgw[:], mcol[:, r_:r_ + 1], None, ALU.is_equal))
                            P.op("vector", lambda e: e.scalar_tensor_tensor(
                                lgw[:], sel[:], -1e30, lgw[:], ALU.mult, ALU.add))
                        P.op("vector", lambda e: e.tensor_scalar(
                            negm0[:], mcol[:, 0:1], -1.0, None, ALU.mult))
                        dvt = P.op("vector", lambda e: e.tensor_scalar(
                            sel[:], lg[:], mcol[:, TOPK - 1:TOPK], None, ALU.is_ge))
                        P.wait("scalar", "dv", dvt)
                        ace = P.op("scalar", lambda e: e.activation(
                            ex[:], lg[:], AF.Exp, bias=negm0[:], scale=1.0))
                        P.wait("vector", "ac", ace)
                        P.op("vector", lambda e: e.tensor_tensor(
                            ex[:], ex[:], sel[:], ALU.mult))
                        P.op("vector", lambda e: e.tensor_reduce(
                            ssum[:, 0:1], ex[:], AX.X, ALU.add))
                        P.op("vector", lambda e: e.reciprocal(
                            ssum[:, 1:2], ssum[:, 0:1]))
                        P.op("vector", lambda e: e.tensor_scalar(
                            ex[:], ex[:], ssum[:, 1:2], None, ALU.mult))
                        P.op("vector", lambda e: e.tensor_copy(cwh[:], ex[:]))
                        dvw = P.op("vector", lambda e: e.tensor_tensor(
                            cwl[:], ex[:], cwh[:], ALU.subtract))
                        P.wait("tensor", "dv", dvw)
                        peh = P.op("tensor", lambda e: e.matmul(
                            ps_cwT[:], cwh[:], idb[:], start=True, stop=True), inc="pe")
                        P.wait("vector", "pe", peh)
                        if ti > 0:
                            P.wait("vector", f"sa{(ti - 1) % 2}", P.c(f"sa{(ti - 1) % 2}"))
                        dvh = P.op("vector", lambda e: e.tensor_copy(tcwh[:], ps_cwT[:]))
                        P.wait("tensor", "dv", dvh)
                        pel = P.op("tensor", lambda e: e.matmul(
                            ps_cwT[:], cwl[:], idb[:], start=True, stop=True), inc="pe")
                        P.wait("vector", "pe", pel)
                        dvc = P.op("vector", lambda e: e.tensor_copy(tcwl[:], ps_cwT[:]))
                        P.wait("gpsimd", "dv", dvc)
                        P.dma("gpsimd", f"sa{ti % 2}", lambda e, ti=ti: e.dma_start(
                            out=cwTdh[:, ti * 128:(ti + 1) * 128], in_=tcwh[:]))
                        P.dma("gpsimd", f"sa{ti % 2}", lambda e, ti=ti: e.dma_start(
                            out=cwTdl[:, ti * 128:(ti + 1) * 128], in_=tcwl[:]),
                            batch_open=True)
                    stage_barrier()

                with ExitStack() as stkC:
                    ps_G = [stkC.enter_context(
                        nc.psum_tensor(f"s7_ps_G{p}_{i}", [128, TG], F32)) for i in range(2)]
                    ps_U = [stkC.enter_context(
                        nc.psum_tensor(f"s7_ps_U{p}_{i}", [128, TG], F32)) for i in range(2)]
                    ps_D = [stkC.enter_context(
                        nc.psum_tensor(f"s7_ps_D{p}_{i}", [128, TG], F32)) for i in range(2)]
                    ps_cwb = [stkC.enter_context(
                        nc.psum_tensor(f"s7_ps_cwb{p}_{i}", [128, TG], F32))
                        for i in range(2)]

                    def expert(e_idx, shared_half):
                        nic = IC
                        if shared_half is None:
                            ed_src = edT[e_idx].rearrange("(ic q) f -> q ic f", q=128)
                            sic0 = 0
                        else:
                            r0 = shared_half * IC * 128
                            ed_src = sdT[r0:r0 + IC * 128, :].rearrange(
                                "(ic q) f -> q ic f", q=128)
                            up_g = None
                            sic0 = shared_half * IC
                        P.wait("gpsimd", "pe", P.c("pe"))
                        lwe = P.dma("gpsimd", "lw1", lambda e, ed_src=ed_src: e.dma_start(
                            out=edres[:].rearrange("p (ic f) -> p ic f", ic=IC),
                            in_=ed_src))
                        if shared_half is None:
                            P.wait("sync", "pe", P.c("pe"))
                            lcw = P.dma("sync", "lc", lambda e, e_idx=e_idx: e.dma_start(
                                out=cwh_r[:], in_=cwTdh[e_idx:e_idx + 1, :]))
                            lcw = P.dma("sync", "lc", lambda e, e_idx=e_idx: e.dma_start(
                                out=cwl_r[:], in_=cwTdl[e_idx:e_idx + 1, :]),
                                batch_open=True)
                            P.wait("tensor", "lc", lcw)
                        def slab_load(ic):
                            gb = egb[ic % 2]
                            ub = eub[ic % 2]
                            if shared_half is None:
                                sg_src = egS[e_idx, ic]
                                su_src = euS[e_idx, ic]
                            else:
                                sg_src = sgS[sic0 + ic]
                                su_src = suS[sic0 + ic]
                            P.dma("gpsimd", "lw0", lambda e, gb=gb, sg_src=sg_src:
                                  e.dma_start(out=gb[:], in_=sg_src[:, :]))
                            return P.dma("gpsimd", "lw0",
                                         lambda e, ub=ub, su_src=su_src:
                                         e.dma_start(out=ub[:], in_=su_src[:, :]),
                                         batch_open=True)
                        slab_pe = {}
                        chain_guard = {}
                        lws_of = {0: slab_load(0)}
                        for ic in range(nic):
                            gb = egb[ic % 2]
                            ub = eub[ic % 2]
                            lws = lws_of[ic]
                            if ic + 1 < nic:
                                if ic >= 1:
                                    P.wait("gpsimd", "pe", slab_pe[ic - 1])
                                lws_of[ic + 1] = slab_load(ic + 1)
                            for tgo in range(2):
                                pG = ps_G[tgo]
                                pU = ps_U[tgo]
                                if ic > 0:
                                    P.wait("tensor", "dv", chain_guard[(ic - 1, tgo)])
                                P.wait("tensor", "lw0", lws)
                                for c in range(FC):
                                    P.op("tensor", lambda e, pG=pG, gb=gb, c=c, tgo=tgo:
                                         e.matmul(pG[:],
                                                  gb[:, c * 128:(c + 1) * 128],
                                                  tfm[:, c * TP + tgo * TG:
                                                      c * TP + tgo * TG + TG],
                                                  start=(c == 0), stop=(c == FC - 1)))
                                    P.op("tensor", lambda e, pU=pU, ub=ub, c=c, tgo=tgo:
                                         e.matmul(pU[:],
                                                  ub[:, c * 128:(c + 1) * 128],
                                                  tfm[:, c * TP + tgo * TG:
                                                      c * TP + tgo * TG + TG],
                                                  start=(c == 0), stop=(c == FC - 1)),
                                         inc="pe" if c == FC - 1 else None)
                                pe_gu = P.c("pe")
                                if shared_half is None and ic == 0:
                                    csl = slice(tgo * TG, tgo * TG + TG)
                                    P.wait("tensor", "dv", P.c("dv"))
                                    P.op("tensor", lambda e, csl=csl, tgo=tgo: e.matmul(
                                        ps_cwb[tgo][:], ones1[:], cwh_r[0:1, csl],
                                        start=True, stop=False))
                                    pcw = P.op("tensor", lambda e, csl=csl, tgo=tgo:
                                               e.matmul(ps_cwb[tgo][:], ones1[:],
                                                        cwl_r[0:1, csl],
                                                        start=False, stop=True), inc="pe")
                                    pe_gu = pcw
                                P.wait("scalar", "pe", pe_gu)
                                prev_key = (ic, tgo - 1) if tgo else (ic - 1, 1)
                                if prev_key in chain_guard:
                                    P.wait("scalar", "dv", chain_guard[prev_key])
                                else:
                                    P.wait("scalar", "dv", P.c("dv"))
                                acs = P.op("scalar", lambda e, pG=pG: e.activation(
                                    sgt[:], pG[:], AF.Silu))
                                P.wait("vector", "ac", acs)
                                P.op("vector", lambda e, pU=pU: e.tensor_tensor(
                                    h1t[:], sgt[:], pU[:], ALU.mult))
                                hsl = slice(ic * TG, (ic + 1) * TG)
                                if shared_half is None:
                                    P.op("vector", lambda e, hsl=hsl, tgo=tgo:
                                         e.tensor_tensor(hh[tgo][:, hsl], h1t[:],
                                                         ps_cwb[tgo][:], ALU.mult))
                                else:
                                    P.op("vector", lambda e, hsl=hsl, tgo=tgo:
                                         e.tensor_copy(hh[tgo][:, hsl], h1t[:]))
                                chain_guard[(ic, tgo)] = P.c("dv")
                                slab_pe[ic] = pe_gu
                        dv_hh = P.c("dv")
                        P.wait("tensor", "dv", dv_hh)
                        P.wait("tensor", "lw1", lwe)
                        dctx = {}
                        for tgo in range(2):
                            for fc in range(FC):
                                k = tgo * FC + fc
                                pD = ps_D[k % 2]
                                if k >= 2:
                                    P.wait("tensor", "dv", dctx[k - 2])
                                for ic in range(nic):
                                    P.op("tensor", lambda e, pD=pD, ic=ic, fc=fc, tgo=tgo:
                                         e.matmul(pD[:],
                                                  edres[:, ic * D + fc * 128:
                                                        ic * D + fc * 128 + 128],
                                                  hh[tgo][:, ic * TG:(ic + 1) * TG],
                                                  start=(ic == 0), stop=(ic == nic - 1)),
                                         inc="pe" if ic == nic - 1 else None)
                                pe_d = P.c("pe")
                                P.wait("vector", "pe", pe_d)
                                asl = slice(fc * TP + tgo * TG, fc * TP + tgo * TG + TG)
                                dva = P.op("vector", lambda e, pD=pD, asl=asl:
                                           e.tensor_tensor(accb[:, asl], pD[:],
                                                           accb[:, asl], ALU.add))
                                dctx[k] = dva

                    for e_idx in range(E_EXP):
                        expert(e_idx, None)
                    expert(None, 0)
                    expert(None, 1)
                    dv_fin = P.c("dv")
                    P.wait("gpsimd", "dv", dv_fin)
                    for tgo in range(2):
                        srcv = accb[:].rearrange("p (c u) -> p c u", u=TP)[
                            :, :, tgo * TG:(tgo + 1) * TG]
                        dsto = out[:, col0 + tgo * TG:col0 + tgo * TG + TG].rearrange(
                            "(c p) t -> p c t", p=128)
                        P.dma("gpsimd", f"sa{tgo}", lambda e, srcv=srcv, dsto=dsto:
                              e.dma_start(out=dsto, in_=srcv))
                    stage_barrier()

        if N_STAGES >= 7:
            stage7(0)
            stage7(1)

        for s in ("sa0", "sa1", "sy", "lw0", "lw1", "la0", "la1", "lc",
                  "pe", "dv", "ac", "po"):
            P.wait("sync", s, P.c(s))

        with nc.Block() as block:
            P.replay(block, sems)

    print("instruction counts:", P.n_ops, "total", sum(P.n_ops.values()))
    return nc



OWN_STRIPES = [[0, 3, 4, 7], [1, 2, 5, 6]]


def make_masks(parity):
    tab = np.zeros((4, 32, 128, TG), np.float32)
    stripes = OWN_STRIPES[parity]
    for j, s in enumerate(stripes):
        for kt in range(KTP[j]):
            kg = kt * 128 + np.arange(128)[:, None]
            qg = s * TG + np.arange(TG)[None, :]
            tab[j, kt] = np.where(kg <= qg, 0.0, NEG)
    return tab.astype(ml_dtypes.bfloat16)


def prep_inputs(inputs):
    g = lambda k: np.asarray(inputs[k])
    x = g("L0_attn_hidden_states")
    xf = [np.ascontiguousarray(x[b].T) for b in range(4)]
    def wswz(a):
        n = a.shape[1]
        return np.ascontiguousarray(
            a.reshape(FC, 128, n).transpose(1, 0, 2).reshape(128, FC * n))

    def upswz(a):
        E0 = a.shape[0]
        return np.ascontiguousarray(
            a.reshape(E0, FC, 128, IC, 128).transpose(0, 3, 2, 1, 4).reshape(
                E0, IC, 128, FC * 128))

    def shswz(a):
        return np.ascontiguousarray(
            a.reshape(FC, 128, SIC, 128).transpose(2, 1, 0, 3).reshape(
                SIC, 128, FC * 128))

    shared = {
        "qwS": wswz(np.ascontiguousarray(g("L0_attn_q_w").T)),
        "kwS": wswz(np.ascontiguousarray(g("L0_attn_k_w").T)),
        "vwS": wswz(np.ascontiguousarray(g("L0_attn_v_w").T)),
        "owS": wswz(np.ascontiguousarray(g("L0_attn_o_w").T)),
        "wqS": wswz(g("L0_ffn_wq")),
        "wkS": wswz(g("L0_ffn_wk")),
        "wvS": wswz(g("L0_ffn_wv")),
        "woS": wswz(g("L0_ffn_wo")),
        "gwS": wswz(np.ascontiguousarray(g("L0_ffn_gate_w").T)),
        "egS": upswz(g("L0_ffn_eg_w").transpose(0, 2, 1)),
        "euS": upswz(g("L0_ffn_eu_w").transpose(0, 2, 1)),
        "edT": np.ascontiguousarray(g("L0_ffn_ed_w").transpose(0, 2, 1)),
        "sgS": shswz(g("L0_ffn_sg_w").T),
        "suS": shswz(g("L0_ffn_su_w").T),
        "sdT": np.ascontiguousarray(g("L0_ffn_sd_w").T),
        "qb2": np.ascontiguousarray(g("L0_attn_q_b").reshape(FC, 128).T),
        "ob2": np.ascontiguousarray(g("L0_attn_o_b").reshape(FC, 128).T),
        "kbr": np.ascontiguousarray(g("L0_attn_k_b").reshape(1, D)),
        "vbr": np.ascontiguousarray(g("L0_attn_v_b").reshape(1, D)),
        "ln1c": np.ascontiguousarray(g("L0_ffn_ln1_w").reshape(FC, 128).T),
        "ln2c": np.ascontiguousarray(g("L0_ffn_ln2_w").reshape(FC, 128).T),
        "idenb": np.eye(128, dtype=ml_dtypes.bfloat16),
    }
    masks = [make_masks(0), make_masks(1)]
    in_maps = []
    for core in range(8):
        b, p = core // 2, core % 2
        stripes = OWN_STRIPES[p]
        xo = np.concatenate([xf[b][:, s * TG:(s + 1) * TG] for s in stripes], axis=1)
        m = dict(shared)
        m["x_full"] = xf[b]
        m["x_own"] = np.ascontiguousarray(xo)
        m["mask_tab"] = masks[p]
        in_maps.append(m)
    return in_maps


def gather_output(results):
    out = np.zeros((4, TF, D), np.float32)
    for core in range(8):
        b, p = core // 2, core % 2
        o = np.asarray(results[core]["out"], np.float32)
        for j, s in enumerate(OWN_STRIPES[p]):
            out[b, s * TG:(s + 1) * TG, :] = o[:, j * TG:(j + 1) * TG].T
    return out


_CACHED = {}


def kernel(**inputs):
    if "nc" not in _CACHED:
        _CACHED["nc"] = build_program()
    nc = _CACHED["nc"]
    in_maps = prep_inputs(inputs)
    res = run_bass_kernel_spmd(nc, in_maps, core_ids=list(range(8)))
    _CACHED["res"] = res
    return gather_output(res.results)


# version 06 (final)
# speedup vs baseline: 1.3104x; 1.3104x over previous
import numpy as np
import ml_dtypes
from contextlib import ExitStack
import concourse.bass as bass
import concourse.mybir as mybir
from concourse.bass_utils import run_bass_kernel_spmd

F32 = mybir.dt.float32
BF16 = mybir.dt.bfloat16
AF = mybir.ActivationFunctionType
ALU = mybir.AluOpType
AX = mybir.AxisListType

D = 2048
TF = 4096
TO = 2048
FC = 16
TG = 512
H = 16
HD = 128
E_EXP = 16
I_DIM = 1408
IC = 11
SI = 2816
SIC = 22
TOPK = 6
SCALE = float(1.0 / np.sqrt(128.0))
KTP = [8, 16, 24, 32]
LN_EPS = 1e-5
RMS_EPS = 1e-6
LA_EPS = 1e-6
NEG = -1e9

ENGINES = ("tensor", "vector", "scalar", "gpsimd", "sync")

N_STAGES = 99
SKIP_EARLY = False
DEBUG = False


class Prog:
    def __init__(self):
        self.streams = {e: [] for e in ENGINES}
        self.counts = {}
        self.inflight = {}
        self.n_ops = {e: 0 for e in ENGINES}

    def c(self, sem):
        return self.counts.get(sem, 0)

    def wait(self, eng, sem, val):
        assert val <= self.c(sem), f"wait {sem}>={val} unreachable (cur {self.c(sem)})"
        if val > 0:
            self.streams[eng].append(("w", sem, val))

    CHAIN = {"vector": "dv", "scalar": "ac", "gpsimd": "po"}

    def op(self, eng, fn, inc=None, amt=1):
        ch = self.CHAIN.get(eng)
        if ch is not None and not (inc and inc.startswith(("sa", "la", "lw", "lc", "sy", "bar", "barg"))):
            prior = self.c(ch)
            if prior > 0:
                self.streams[eng].append(("w", ch, prior))
            assert inc is None or inc == ch, f"op on {eng} must inc {ch}, got {inc}"
            inc, amt = ch, 1
        if inc is not None:
            self.counts[inc] = self.c(inc) + amt
        self.streams[eng].append(("o", fn, inc, amt))
        self.n_ops[eng] += 1
        return self.counts.get(inc) if inc else None

    def dma(self, eng, sem, fn, batch_open=False):
        if not batch_open and self.inflight.get(sem, 0) > 0:
            self.wait(eng, sem, self.c(sem))
            self.inflight[sem] = 0
        self.inflight[sem] = self.inflight.get(sem, 0) + 1
        return self.op(eng, fn, inc=sem, amt=16)

    def replay(self, block, sems):
        for eng in ENGINES:
            stream = self.streams[eng]

            def make(stream):
                def body(e):
                    for it in stream:
                        if it[0] == "w":
                            e.wait_ge(sems[it[1]], it[2])
                        else:
                            ins = it[1](e)
                            if it[2] is not None:
                                ins.then_inc(sems[it[2]], it[3])
                return body

            getattr(block, eng)(make(stream))


def build_program():
    nc = bass.Bass("TRN2", target_bir_lowering=False, debug=False)

    def inp(name, shape, dt=F32):
        return nc.dram_tensor(name, shape, dt, kind="ExternalInput").ap()

    x_full = inp("x_full", [D, TF])
    x_own = inp("x_own", [D, TO])
    qwS = inp("qwS", [128, FC * D])
    kwS = inp("kwS", [128, FC * D])
    vwS = inp("vwS", [128, FC * D])
    owS = inp("owS", [128, FC * D])
    wqS = inp("wqS", [128, FC * D])
    wkS = inp("wkS", [128, FC * D])
    wvS = inp("wvS", [128, FC * D])
    woS = inp("woS", [128, FC * D])
    gwS = inp("gwS", [128, FC * E_EXP])
    egS = inp("egS", [E_EXP, IC, 128, FC * 128])
    euS = inp("euS", [E_EXP, IC, 128, FC * 128])
    edT = inp("edT", [E_EXP, I_DIM, D])
    sgS = inp("sgS", [SIC, 128, FC * 128])
    suS = inp("suS", [SIC, 128, FC * 128])
    sdT = inp("sdT", [SI, D])
    qb2 = inp("qb2", [128, FC])
    ob2 = inp("ob2", [128, FC])
    kbr = inp("kbr", [1, D])
    vbr = inp("vbr", [1, D])
    ln1c = inp("ln1c", [128, FC])
    ln2c = inp("ln2c", [128, FC])
    idenb = inp("idenb", [128, 128], BF16)
    mask_tab = inp("mask_tab", [4, 32, 128, TG], BF16)

    def scratch(name, shape, dt=F32):
        kind = "ExternalOutput" if DEBUG else "Internal"
        return nc.dram_tensor(name, shape, dt, kind=kind).ap()

    xnb = scratch("xnb", [D, TF], BF16)
    xnob = scratch("xnob", [D, TO], BF16)
    xnof = scratch("xnof", [D, TO], F32)
    hfm = scratch("hfm", [D, TF], BF16)
    kfm = scratch("kfm", [D, TF], BF16)
    vtm = scratch("vtm", [H, TF, HD], BF16)
    qfm = scratch("qfm", [D, TO], BF16)
    acc1 = scratch("acc1", [D, TO], F32)
    x2d = scratch("x2d", [D, TO], F32)
    cwTdh = scratch("cwTdh", [E_EXP, TO // 2], BF16)
    cwTdl = scratch("cwTdl", [E_EXP, TO // 2], BF16)
    out = nc.dram_tensor("out", [D, TO], F32, kind="ExternalOutput").ap()

    P = Prog()
    SEMS = ["la0", "la1", "lc", "lw0", "lw1", "sa0", "sa1", "sy", "pe", "dv", "ac", "po", "bar", "barg"]

    with ExitStack() as topstack:
        en = topstack.enter_context
        idb = en(nc.sbuf_tensor("idb", [128, 128], BF16))
        ones1 = en(nc.sbuf_tensor("ones1", [1, 128], BF16))
        ones8 = en(nc.sbuf_tensor("ones8", [128, 8], BF16))
        eps_ln = en(nc.sbuf_tensor("eps_ln", [1, 1], F32))
        eps_rms = en(nc.sbuf_tensor("eps_rms", [1, 1], F32))
        qb2s = en(nc.sbuf_tensor("qb2s", [128, FC], F32))
        ob2s = en(nc.sbuf_tensor("ob2s", [128, FC], F32))
        ln1s = en(nc.sbuf_tensor("ln1s", [128, FC], F32))
        ln2s = en(nc.sbuf_tensor("ln2s", [128, FC], F32))
        la_stack = ExitStack()
        kvb = la_stack.enter_context(nc.sbuf_tensor("kvb", [128, H * HD], BF16))
        ksb_h = la_stack.enter_context(nc.sbuf_tensor("ksb_h", [128, 128], BF16))
        ksb_l = la_stack.enter_context(nc.sbuf_tensor("ksb_l", [128, 128], BF16))
        sems = {s: en(nc.semaphore(s)) for s in SEMS}

        P.dma("sync", "lc", lambda e: e.dma_start(out=idb[:], in_=idenb[:, :]))
        P.dma("sync", "lc", lambda e: e.dma_start(out=qb2s[:], in_=qb2[:, :]), batch_open=True)
        P.dma("sync", "lc", lambda e: e.dma_start(out=ob2s[:], in_=ob2[:, :]), batch_open=True)
        P.dma("sync", "lc", lambda e: e.dma_start(out=ln1s[:], in_=ln1c[:, :]), batch_open=True)
        P.dma("sync", "lc", lambda e: e.dma_start(out=ln2s[:], in_=ln2c[:, :]), batch_open=True)
        P.op("vector", lambda e: e.memset(ones1[:], 1.0))
        P.op("vector", lambda e: e.memset(ones8[:], 1.0))
        P.op("vector", lambda e: e.memset(eps_ln[:], LN_EPS))
        const_dv = P.op("vector", lambda e: e.memset(eps_rms[:], RMS_EPS), inc="dv")
        const_la = P.c("lc")
        for eng in ("scalar", "vector", "tensor", "gpsimd"):
            P.wait(eng, "lc", const_la)
        for eng in ("scalar", "tensor", "gpsimd", "sync"):
            P.wait(eng, "dv", const_dv)

        def stage_barrier():
            dma_tot = {s: P.c(s) for s in ("la0", "la1", "lc", "sy",
                                           "lw0", "lw1", "sa0", "sa1")}
            mil = {s: P.c(s) for s in ("pe", "dv", "ac", "po")}
            for eng in ENGINES:
                for s, v in dma_tot.items():
                    P.wait(eng, s, v)
                for s, v in mil.items():
                    P.wait(eng, s, v)
            ack = {}
            ack["pe"] = P.op("tensor", lambda e: e.nop(), inc="pe")
            ack["dv"] = P.op("vector", lambda e: e.nop())
            ack["dv"] = P.c("dv")
            ack["ac"] = P.op("scalar", lambda e: e.nop())
            ack["ac"] = P.c("ac")
            ack["po"] = P.op("gpsimd", lambda e: e.nop())
            ack["po"] = P.c("po")
            for eng in ("sync", "gpsimd"):
                for s in ("pe", "dv", "ac", "po"):
                    P.wait(eng, s, ack[s])

        def bcast_psum(psum_tile, hi, lo, pe_inc=True):
            P.op("tensor", lambda e: e.matmul(psum_tile, ones1[:], hi, start=True, stop=False))
            return P.op("tensor", lambda e: e.matmul(psum_tile, ones1[:], lo, start=False,
                                                     stop=True), inc="pe" if pe_inc else None)

        def stage_ln(pref, src_dram, Tlen, dst_bf, dst_f32):
            with ExitStack() as stk:
                sen = stk.enter_context
                xt = [sen(nc.sbuf_tensor(f"{pref}_xt{i}", [128, FC * TG], F32))
                      for i in range(2)]
                xb = [sen(nc.sbuf_tensor(f"{pref}_xb{i}", [128, FC * TG], BF16))
                      for i in range(2)]
                xq = [sen(nc.sbuf_tensor(f"{pref}_xq{i}", [128, FC * TG], BF16))
                      for i in range(2)]
                xnbuf = sen(nc.sbuf_tensor(f"{pref}_xnb", [128, FC * TG], BF16))
                xnf = sen(nc.sbuf_tensor(f"{pref}_xnf", [128, FC * TG], F32))
                rows = sen(nc.sbuf_tensor(f"{pref}_rows", [1, 6 * TG], F32))
                ab_h = sen(nc.sbuf_tensor(f"{pref}_abh", [1, TG], BF16))
                ab_l = sen(nc.sbuf_tensor(f"{pref}_abl", [1, TG], BF16))
                bb_h = sen(nc.sbuf_tensor(f"{pref}_bbh", [1, TG], BF16))
                bb_l = sen(nc.sbuf_tensor(f"{pref}_bbl", [1, TG], BF16))
                ps_s = sen(nc.psum_tensor(f"{pref}_ps_s", [8, TG], F32))
                ps_q = sen(nc.psum_tensor(f"{pref}_ps_q", [8, TG], F32))
                ps_a = sen(nc.psum_tensor(f"{pref}_ps_a", [128, TG], F32))
                ps_b = sen(nc.psum_tensor(f"{pref}_ps_b", [128, TG], F32))

                ntg = Tlen // TG
                hist = []
                for tg in range(ntg):
                    buf = xt[tg % 2]
                    src = src_dram[:, tg * TG:(tg + 1) * TG].rearrange(
                        "(c p) t -> p c t", p=128)
                    if tg >= 2:
                        P.wait("sync", "ac", hist[tg - 2][0])
                        P.wait("sync", "dv", hist[tg - 2][1])
                    lsem = f"la{tg % 2}"
                    la = P.dma("sync", lsem, lambda e, buf=buf, src=src: e.dma_start(
                        out=buf[:].rearrange("p (c t) -> p c t", c=FC), in_=src))
                    P.wait("scalar", lsem, la)
                    xbb = xb[tg % 2]
                    xqb = xq[tg % 2]
                    if tg >= 2:
                        P.wait("scalar", "pe", hist[tg - 2][3])
                    P.op("scalar", lambda e, buf=buf, xbb=xbb: e.activation(
                        xbb[:], buf[:], AF.Identity))
                    ac1 = P.op("scalar", lambda e, buf=buf, xqb=xqb: e.activation(
                        xqb[:], buf[:], AF.Square), inc="ac")
                    P.wait("tensor", "ac", ac1)
                    if tg >= 1:
                        P.wait("tensor", "dv", hist[tg - 1][4])
                    for c in range(FC):
                        P.op("tensor", lambda e, c=c, xbb=xbb: e.matmul(
                            ps_s[:], ones8[:], xbb[:, c * TG:(c + 1) * TG],
                            start=(c == 0), stop=(c == FC - 1)))
                        P.op("tensor", lambda e, c=c, xqb=xqb: e.matmul(
                            ps_q[:], ones8[:], xqb[:, c * TG:(c + 1) * TG],
                            start=(c == 0), stop=(c == FC - 1)),
                            inc="pe" if c == FC - 1 else None)
                    pe_stats = P.c("pe")
                    mu = rows[:, 0:TG]
                    msq = rows[:, TG:2 * TG]
                    mu2 = rows[:, 2 * TG:3 * TG]
                    var = rows[:, 3 * TG:4 * TG]
                    sd = rows[:, 4 * TG:5 * TG]
                    rs = rows[:, 5 * TG:6 * TG]
                    P.wait("vector", "pe", pe_stats)
                    P.op("vector", lambda e, mu=mu: e.tensor_scalar(
                        mu, ps_s[0:1, :], 1.0 / D, None, ALU.mult))
                    P.op("vector", lambda e, msq=msq: e.tensor_scalar(
                        msq, ps_q[0:1, :], 1.0 / D, None, ALU.mult))
                    P.op("vector", lambda e, mu=mu, mu2=mu2: e.tensor_tensor(
                        mu2, mu, mu, ALU.mult))
                    dv1 = P.op("vector", lambda e, msq=msq, mu2=mu2, var=var: e.tensor_tensor(
                        var, msq, mu2, ALU.subtract), inc="dv")
                    dv_rows = dv1
                    P.wait("scalar", "dv", dv1)
                    ac2 = P.op("scalar", lambda e, var=var, sd=sd: e.activation(
                        sd, var, AF.Sqrt, bias=eps_ln[:]), inc="ac")
                    P.wait("vector", "ac", ac2)
                    dv2 = P.op("vector", lambda e, sd=sd, rs=rs: e.reciprocal(rs, sd),
                               inc="dv")
                    P.wait("vector", "dv", dv2)
                    P.op("vector", lambda e, rs=rs: e.tensor_copy(ab_h[:], rs))
                    P.op("vector", lambda e, rs=rs: e.tensor_tensor(
                        ab_l[:], rs, ab_h[:], ALU.subtract))
                    P.op("vector", lambda e, mu=mu, rs=rs, mu2=mu2: e.scalar_tensor_tensor(
                        mu2, mu, -1.0, rs, ALU.mult, ALU.mult))
                    P.op("vector", lambda e, mu2=mu2: e.tensor_copy(bb_h[:], mu2))
                    dv3 = P.op("vector", lambda e, mu2=mu2: e.tensor_tensor(
                        bb_l[:], mu2, bb_h[:], ALU.subtract), inc="dv")
                    P.wait("tensor", "dv", dv3)
                    bcast_psum(ps_a[:], ab_h[:], ab_l[:], pe_inc=False)
                    pe_b = bcast_psum(ps_b[:], bb_h[:], bb_l[:])
                    P.wait("vector", "pe", pe_b)
                    if tg >= 1:
                        P.wait("vector", hist[tg - 1][2][0], hist[tg - 1][2][1])
                    for c in range(FC):
                        sl = slice(c * TG, (c + 1) * TG)
                        P.op("vector", lambda e, sl=sl, buf=buf: e.tensor_tensor(
                            xnf[:, sl], buf[:, sl], ps_a[:], ALU.mult))
                        if dst_f32 is None:
                            P.op("vector", lambda e, sl=sl: e.tensor_tensor(
                                xnbuf[:, sl], xnf[:, sl], ps_b[:], ALU.add))
                        else:
                            P.op("vector", lambda e, sl=sl: e.tensor_tensor(
                                xnf[:, sl], xnf[:, sl], ps_b[:], ALU.add))
                    dv4 = P.c("dv")
                    ssem = f"sa{tg % 2}"
                    if dst_f32 is not None:
                        P.wait("scalar", "dv", dv4)
                        if tg >= 1:
                            P.wait("scalar", hist[tg - 1][2][0], hist[tg - 1][2][1])
                        ac3 = P.op("scalar", lambda e: e.activation(
                            xnbuf[:], xnf[:], AF.Identity), inc="ac")
                        P.wait("gpsimd", "ac", ac3)
                    else:
                        ac3 = P.c("ac")
                        P.wait("gpsimd", "dv", dv4)
                    dstb = dst_bf[:, tg * TG:(tg + 1) * TG].rearrange(
                        "(c p) t -> p c t", p=128)
                    P.dma("gpsimd", ssem, lambda e, dstb=dstb: e.dma_start(
                        out=dstb, in_=xnbuf[:].rearrange("p (c t) -> p c t", c=FC)))
                    if dst_f32 is not None:
                        dstf = dst_f32[:, tg * TG:(tg + 1) * TG].rearrange(
                            "(c p) t -> p c t", p=128)
                        P.wait("gpsimd", "dv", dv4)
                        P.dma("gpsimd", ssem, lambda e, dstf=dstf: e.dma_start(
                            out=dstf, in_=xnf[:].rearrange("p (c t) -> p c t", c=FC)),
                            batch_open=True)
                    hist.append((ac1, dv4, (ssem, P.c(ssem)), pe_stats, dv_rows))
                stage_barrier()

        if N_STAGES >= 1 and not SKIP_EARLY:
            stage_ln("sA", x_full, TF, xnb, None)
            stage_ln("sB", x_own, TO, xnob, xnof)

        def stage2():
            with ExitStack() as stk:
                sen = stk.enter_context
                kws = sen(nc.sbuf_tensor("s2_kws", [128, FC * D], BF16))
                vws = sen(nc.sbuf_tensor("s2_vws", [128, FC * D], BF16))
                kbb = sen(nc.sbuf_tensor("s2_kbb", [128, D], F32))
                vbb = sen(nc.sbuf_tensor("s2_vbb", [128, D], F32))
                kbrs = sen(nc.sbuf_tensor("s2_kbrs", [1, D], F32))
                vbrs = sen(nc.sbuf_tensor("s2_vbrs", [1, D], F32))
                kb_h = sen(nc.sbuf_tensor("s2_kbh", [1, D], BF16))
                kb_l = sen(nc.sbuf_tensor("s2_kbl", [1, D], BF16))
                vb_h = sen(nc.sbuf_tensor("s2_vbh", [1, D], BF16))
                vb_l = sen(nc.sbuf_tensor("s2_vbl", [1, D], BF16))
                xnt = [sen(nc.sbuf_tensor(f"s2_xnt{i}", [128, FC * 128], BF16))
                       for i in range(2)]
                kpt = sen(nc.sbuf_tensor("s2_kpt", [128, D], BF16))
                vvt = sen(nc.sbuf_tensor("s2_vvt", [128, D], BF16))
                t1 = sen(nc.sbuf_tensor("s2_t1", [128, TG], F32))
                t2 = sen(nc.sbuf_tensor("s2_t2", [128, TG], F32))
                t3 = sen(nc.sbuf_tensor("s2_t3", [128, TG], F32))
                ksf = sen(nc.sbuf_tensor("s2_ksf", [128, 128], F32))
                ps_k = sen(nc.psum_tensor("s2_ps_k", [128, TG], F32))
                ps_v = sen(nc.psum_tensor("s2_ps_v", [128, TG], F32))
                ps_kv = [sen(nc.psum_tensor(f"s2_ps_kv{i}", [128, TG], F32))
                         for i in range(4)]
                ps_ks = sen(nc.psum_tensor("s2_ps_ks", [128, 128], F32))

                kws3 = kws[:].rearrange("p (c e) -> p c e", c=FC)
                vws3 = vws[:].rearrange("p (c e) -> p c e", c=FC)
                P.dma("gpsimd", "lw0", lambda e: e.dma_start(out=kws[:], in_=kwS[:, :]))
                P.dma("gpsimd", "lw0", lambda e: e.dma_start(out=vws[:], in_=vwS[:, :]),
                      batch_open=True)
                lw_w = P.c("lw0")
                P.dma("sync", "lc", lambda e: e.dma_start(out=kbrs[:], in_=kbr[:, :]))
                la_rows = P.dma("sync", "lc", lambda e: e.dma_start(
                    out=vbrs[:], in_=vbr[:, :]), batch_open=True)
                P.wait("vector", "lc", la_rows)
                P.op("vector", lambda e: e.tensor_copy(kb_h[:], kbrs[:]))
                P.op("vector", lambda e: e.tensor_tensor(kb_l[:], kbrs[:], kb_h[:],
                                                         ALU.subtract))
                P.op("vector", lambda e: e.tensor_copy(vb_h[:], vbrs[:]))
                dvr = P.op("vector", lambda e: e.tensor_tensor(
                    vb_l[:], vbrs[:], vb_h[:], ALU.subtract), inc="dv")
                P.wait("tensor", "dv", dvr)
                for egi in range(4):
                    esl = slice(egi * TG, (egi + 1) * TG)
                    if egi > 0:
                        P.wait("tensor", "ac", P.c("ac"))
                    pe1 = bcast_psum(ps_k[:], kb_h[0:1, esl], kb_l[0:1, esl])
                    P.wait("scalar", "pe", pe1)
                    P.op("scalar", lambda e, esl=esl: e.activation(
                        kbb[:, esl], ps_k[:], AF.Identity), inc="ac")
                    pe2 = bcast_psum(ps_v[:], vb_h[0:1, esl], vb_l[0:1, esl])
                    P.wait("scalar", "pe", pe2)
                    P.op("scalar", lambda e, esl=esl: e.activation(
                        vbb[:, esl], ps_v[:], AF.Identity), inc="ac")
                ac_bias = P.c("ac")
                P.wait("tensor", "ac", ac_bias)

                ntc = TF // 128
                for tc in range(ntc):
                    buf = xnt[tc % 2]
                    src = xnb[:, tc * 128:(tc + 1) * 128].rearrange("(c p) t -> p c t", p=128)
                    if tc >= 2:
                        P.wait("sync", "pe", P.c("pe"))
                    lsem = f"la{tc % 2}"
                    la = P.dma("sync", lsem, lambda e, buf=buf, src=src: e.dma_start(
                        out=buf[:].rearrange("p (c t) -> p c t", c=FC), in_=src))
                    P.wait("tensor", lsem, la)
                    if tc == 0:
                        P.wait("tensor", "lw0", lw_w)
                    elu_end = {}
                    for egi in range(4):
                        esl = slice(egi * TG, (egi + 1) * TG)
                        for c in range(FC):
                            lhs = buf[:, c * 128:(c + 1) * 128]
                            P.op("tensor", lambda e, lhs=lhs, c=c, egi=egi: e.matmul(
                                ps_k[:], lhs, kws3[:, c, egi * TG:(egi + 1) * TG],
                                start=(c == 0), stop=(c == FC - 1)))
                            P.op("tensor", lambda e, lhs=lhs, c=c, egi=egi: e.matmul(
                                ps_v[:], lhs, vws3[:, c, egi * TG:(egi + 1) * TG],
                                start=(c == 0), stop=(c == FC - 1)),
                                inc="pe" if c == FC - 1 else None)
                        pe_kv = P.c("pe")
                        P.wait("vector", "pe", pe_kv)
                        P.op("vector", lambda e, esl=esl: e.tensor_tensor(
                            vvt[:, esl], ps_v[:], vbb[:, esl], ALU.add))
                        dv_free = P.op("vector", lambda e, esl=esl: e.tensor_tensor(
                            t1[:], ps_k[:], kbb[:, esl], ALU.add))
                        P.op("vector", lambda e: e.tensor_scalar(
                            t2[:], t1[:], 0.0, None, ALU.min))
                        dv1 = P.op("vector", lambda e: e.tensor_scalar(
                            t3[:], t1[:], 0.0, None, ALU.max), inc="dv")
                        P.wait("scalar", "dv", dv1)
                        ac1 = P.op("scalar", lambda e: e.activation(t2[:], t2[:], AF.Exp),
                                   inc="ac")
                        P.wait("vector", "ac", ac1)
                        dv2 = P.op("vector", lambda e, esl=esl: e.tensor_tensor(
                            kpt[:, esl], t3[:], t2[:], ALU.add), inc="dv")
                        P.wait("tensor", "dv", dv_free)
                        elu_end[esl.start] = dv2
                    P.wait("tensor", "dv", max(elu_end.values()))
                    for h in range(H):
                        hsl = slice(h * HD, (h + 1) * HD)
                        P.op("tensor", lambda e, hsl=hsl, tc=tc, h=h: e.matmul(
                            ps_kv[h // 4][:, (h % 4) * 128:(h % 4) * 128 + 128],
                            kpt[:, hsl], vvt[:, hsl],
                            start=(tc == 0 and h % 4 == 0),
                            stop=(tc == ntc - 1 and h % 4 == 3)))
                        P.op("tensor", lambda e, hsl=hsl, tc=tc, h=h: e.matmul(
                            ps_ks[:, h * 8:h * 8 + 8], kpt[:, hsl], ones8[:],
                            start=(tc == 0 and h == 0),
                            stop=(tc == ntc - 1 and h == H - 1)),
                            inc="pe" if (tc == ntc - 1 and h == H - 1) else None)
                pe_fin = P.c("pe")
                P.wait("scalar", "pe", pe_fin)
                for i in range(4):
                    P.op("scalar", lambda e, i=i: e.activation(
                        kvb[:, i * TG:(i + 1) * TG], ps_kv[i][:], AF.Identity))
                ac_kv = P.op("scalar", lambda e: e.activation(
                    ksf[:], ps_ks[:], AF.Identity), inc="ac")
                P.wait("vector", "ac", ac_kv)
                P.op("vector", lambda e: e.tensor_copy(ksb_h[:], ksf[:]))
                dv_ks = P.op("vector", lambda e: e.tensor_tensor(
                    ksb_l[:], ksf[:], ksb_h[:], ALU.subtract), inc="dv")
                stage_barrier()
                return dv_ks

        dv_ks = stage2() if (N_STAGES >= 2 and not SKIP_EARLY) else 0

        def stage3():
            with ExitStack() as stk:
                sen = stk.enter_context
                qws = sen(nc.sbuf_tensor("s3_qws", [128, FC * D], BF16))
                attn_all = sen(nc.sbuf_tensor("s3_attn", [128, H * 4 * TG], BF16))
                ps_q = [sen(nc.psum_tensor(f"s3_ps_q{i}", [128, TG], F32)) for i in range(2)]
                ps_a = [sen(nc.psum_tensor(f"s3_ps_a{i}", [128, TG], F32)) for i in range(2)]
                ps_z = sen(nc.psum_tensor("s3_ps_z", [8, TG], F32))
                ps_zr = sen(nc.psum_tensor("s3_ps_zr", [128, TG], F32))
                stk3a = ExitStack()
                s3n = stk3a.enter_context
                xnt = [s3n(nc.sbuf_tensor(f"s3_xnt{i}", [128, FC * TG], BF16))
                       for i in range(2)]
                qpt = s3n(nc.sbuf_tensor("s3_qpt", [128, FC * TG], BF16))
                t1 = s3n(nc.sbuf_tensor("s3_t1", [128, TG], F32))
                t2 = s3n(nc.sbuf_tensor("s3_t2", [128, TG], F32))
                t3 = s3n(nc.sbuf_tensor("s3_t3", [128, TG], F32))
                zrow = s3n(nc.sbuf_tensor("s3_zrow", [1, TG], F32))
                zh = s3n(nc.sbuf_tensor("s3_zh", [1, TG], BF16))
                zl = s3n(nc.sbuf_tensor("s3_zl", [1, TG], BF16))
                zrs = s3n(nc.sbuf_tensor("s3_zrs", [128, TG], F32))

                qws3 = qws[:].rearrange("p (c o) -> p c o", c=FC)
                P.dma("gpsimd", "lw0", lambda e: e.dma_start(out=qws[:], in_=qwS[:, :]))
                lw_q = P.c("lw0")

                for tg in range(TO // TG):
                    buf = xnt[tg % 2]
                    src = xnob[:, tg * TG:(tg + 1) * TG].rearrange("(c p) t -> p c t", p=128)
                    if tg >= 2:
                        P.wait("sync", "pe", P.c("pe"))
                    lsem = f"la{tg % 2}"
                    la = P.dma("sync", lsem, lambda e, buf=buf, src=src: e.dma_start(
                        out=buf[:].rearrange("p (c t) -> p c t", c=FC), in_=src))
                    P.wait("tensor", lsem, la)
                    if tg == 0:
                        P.wait("tensor", "lw0", lw_q)
                    consume_dv = {}
                    for oc in range(FC):
                        pq = ps_q[oc % 2]
                        if oc >= 2:
                            P.wait("tensor", "dv", consume_dv[oc - 2])
                        elif tg > 0:
                            P.wait("tensor", "dv", P.c("dv"))
                        for c in range(FC):
                            lhs = qws3[:, c, oc * 128:(oc + 1) * 128]
                            rhs = buf[:, c * TG:(c + 1) * TG]
                            P.op("tensor", lambda e, pq=pq, lhs=lhs, rhs=rhs, c=c: e.matmul(
                                pq[:], lhs, rhs, start=(c == 0), stop=(c == FC - 1)),
                                inc="pe" if c == FC - 1 else None)
                        pe_q = P.c("pe")
                        P.wait("vector", "pe", pe_q)
                        P.op("vector", lambda e, pq=pq, oc=oc: e.tensor_scalar(
                            t1[:], pq[:], qb2s[:, oc:oc + 1], None, ALU.add))
                        consume_dv[oc] = P.c("dv")
                        P.op("vector", lambda e: e.tensor_scalar(
                            t2[:], t1[:], 0.0, None, ALU.min))
                        dv1 = P.op("vector", lambda e: e.tensor_scalar(
                            t3[:], t1[:], 0.0, None, ALU.max), inc="dv")
                        P.wait("scalar", "dv", dv1)
                        ac1 = P.op("scalar", lambda e: e.activation(t2[:], t2[:], AF.Exp),
                                   inc="ac")
                        P.wait("vector", "ac", ac1)
                        dv2 = P.op("vector", lambda e, oc=oc: e.tensor_tensor(
                            qpt[:, oc * TG:(oc + 1) * TG], t3[:], t2[:], ALU.add), inc="dv")
                        P.wait("tensor", "dv", dv2)
                    if tg == 0 and N_STAGES >= 2:
                        P.wait("tensor", "dv", dv_ks)
                    hdva = {}
                    hzr = {}
                    hacz = {}
                    for h in range(H):
                        pa = ps_a[h % 2]
                        qsl = qpt[:, h * TG:(h + 1) * TG]
                        if h >= 2:
                            P.wait("tensor", "dv", hdva[h - 2])
                        elif tg > 0:
                            P.wait("tensor", "dv", P.c("dv"))
                        P.op("tensor", lambda e, pa=pa, h=h, qsl=qsl: e.matmul(
                            pa[:], kvb[:, h * HD:(h + 1) * HD], qsl, start=True, stop=True))
                        if h >= 1:
                            P.wait("tensor", "dv", hzr[h - 1])
                        P.op("tensor", lambda e, h=h, qsl=qsl: e.matmul(
                            ps_z[:], ksb_h[:, h * 8:h * 8 + 8], qsl, start=True, stop=False))
                        pe_a = P.op("tensor", lambda e, h=h, qsl=qsl: e.matmul(
                            ps_z[:], ksb_l[:, h * 8:h * 8 + 8], qsl, start=False, stop=True),
                            inc="pe")
                        P.wait("vector", "pe", pe_a)
                        hzr[h] = P.op("vector", lambda e: e.tensor_scalar(
                            zrow[:], ps_z[0:1, :], LA_EPS, None, ALU.add))
                        dvz = P.op("vector", lambda e: e.reciprocal(zrow[:], zrow[:]),
                                   inc="dv")
                        P.op("vector", lambda e: e.tensor_copy(zh[:], zrow[:]))
                        dvz2 = P.op("vector", lambda e: e.tensor_tensor(
                            zl[:], zrow[:], zh[:], ALU.subtract), inc="dv")
                        P.wait("tensor", "dv", dvz2)
                        if h >= 1:
                            P.wait("tensor", "ac", hacz[h - 1])
                        pe_zr = bcast_psum(ps_zr[:], zh[:, :], zl[:, :])
                        P.wait("scalar", "pe", pe_zr)
                        if h >= 1:
                            P.wait("scalar", "dv", hdva[h - 1])
                        hacz[h] = P.op("scalar", lambda e: e.activation(
                            zrs[:], ps_zr[:], AF.Identity), inc="ac")
                        P.wait("vector", "ac", hacz[h])
                        asl = slice((h * 4 + tg) * TG, (h * 4 + tg) * TG + TG)
                        hdva[h] = P.op("vector", lambda e, pa=pa, asl=asl: e.tensor_tensor(
                            attn_all[:, asl], pa[:], zrs[:], ALU.mult), inc="dv")

                stage_barrier()
                stk3a.close()
                xot = sen(nc.sbuf_tensor("s3_xot", [128, FC * TG], F32))
                a1t = sen(nc.sbuf_tensor("s3_a1t", [128, FC * TG], F32))
                dv_attn = P.c("dv")
                ows3 = qws3
                P.wait("gpsimd", "pe", P.c("pe"))
                lw_o = P.dma("gpsimd", "lw0", lambda e: e.dma_start(
                    out=qws[:], in_=owS[:, :]))
                P.wait("tensor", "dv", dv_attn)
                sa_hist = None
                for tg in range(TO // TG):
                    src = x_own[:, tg * TG:(tg + 1) * TG].rearrange("(c p) t -> p c t", p=128)
                    if sa_hist is not None:
                        P.wait("sync", sa_hist[0], sa_hist[1])
                    lsem = f"la{tg % 2}"
                    la = P.dma("sync", lsem, lambda e, src=src: e.dma_start(
                        out=xot[:].rearrange("p (c t) -> p c t", c=FC), in_=src))
                    if tg == 0:
                        P.wait("tensor", "lw0", lw_o)
                    evac_dv = {}
                    for oc in range(FC):
                        pq = ps_q[oc % 2]
                        if oc >= 2:
                            P.wait("tensor", "dv", evac_dv[oc - 2])
                        elif tg > 0:
                            P.wait("tensor", "dv", P.c("dv"))
                        for ec in range(FC):
                            lhs = ows3[:, ec, oc * 128:(oc + 1) * 128]
                            rhs = attn_all[:, (ec * 4 + tg) * TG:(ec * 4 + tg) * TG + TG]
                            P.op("tensor", lambda e, pq=pq, lhs=lhs, rhs=rhs, ec=ec: e.matmul(
                                pq[:], lhs, rhs, start=(ec == 0), stop=(ec == FC - 1)),
                                inc="pe" if ec == FC - 1 else None)
                        pe_o = P.c("pe")
                        P.wait("vector", "pe", pe_o)
                        if oc == 0:
                            P.wait("vector", lsem, la)
                            if sa_hist is not None:
                                P.wait("vector", sa_hist[0], sa_hist[1])
                        csl = slice(oc * TG, (oc + 1) * TG)
                        P.op("vector", lambda e, pq=pq, csl=csl: e.tensor_tensor(
                            a1t[:, csl], pq[:], xot[:, csl], ALU.add))
                        P.op("vector", lambda e, csl=csl, oc=oc: e.tensor_scalar(
                            a1t[:, csl], a1t[:, csl], ob2s[:, oc:oc + 1], None, ALU.add))
                        evac_dv[oc] = P.c("dv")
                    dv_o = P.c("dv")
                    P.wait("gpsimd", "dv", dv_o)
                    dsta = acc1[:, tg * TG:(tg + 1) * TG].rearrange("(c p) t -> p c t", p=128)
                    ssem = f"sa{tg % 2}"
                    sa_val = P.dma("gpsimd", ssem, lambda e, dsta=dsta: e.dma_start(
                        out=dsta, in_=a1t[:].rearrange("p (c t) -> p c t", c=FC)))
                    sa_hist = (ssem, sa_val)
                stage_barrier()

        if N_STAGES >= 3 and not SKIP_EARLY:
            stage3()
        la_stack.close()

        def stage4k():
            with ExitStack() as stk:
                sen = stk.enter_context
                kws = sen(nc.sbuf_tensor("s4_kws", [128, FC * D], BF16))
                xnt = [sen(nc.sbuf_tensor(f"s4_xnt{i}", [128, FC * TG], BF16))
                       for i in range(2)]
                hb = sen(nc.sbuf_tensor("s4_hb", [128, FC * TG], BF16))
                kout = sen(nc.sbuf_tensor("s4_kout", [128, FC * TG], BF16))
                xqs = [sen(nc.sbuf_tensor(f"s4_xqs{i}", [128, FC * TG], BF16)) for i in range(2)]
                t1 = sen(nc.sbuf_tensor("s4_t1", [128, TG], F32))
                rows = sen(nc.sbuf_tensor("s4_rows", [1, 3 * TG], F32))
                rs_h = sen(nc.sbuf_tensor("s4_rsh", [1, TG], BF16))
                rs_l = sen(nc.sbuf_tensor("s4_rsl", [1, TG], BF16))
                ps_q = sen(nc.psum_tensor("s4_ps_q", [8, TG], F32))
                ps_r = sen(nc.psum_tensor("s4_ps_r", [128, TG], F32))
                ps_k = [sen(nc.psum_tensor(f"s4_ps_k{i}", [128, TG], F32)) for i in range(2)]

                kws3 = kws[:].rearrange("p (c e) -> p c e", c=FC)
                lw_w = P.dma("gpsimd", "lw0", lambda e: e.dma_start(
                    out=kws[:], in_=wkS[:, :]))
                hist = []
                for tg in range(TF // TG):
                    buf = xnt[tg % 2]
                    srcx = xnb[:, tg * TG:(tg + 1) * TG].rearrange("(c p) t -> p c t", p=128)
                    if tg >= 2:
                        P.wait("sync", "ac", hist[tg - 2][0])
                        P.wait("sync", "dv", hist[tg - 2][3])
                    lsem = f"la{tg % 2}"
                    la = P.dma("sync", lsem, lambda e, buf=buf, srcx=srcx: e.dma_start(
                        out=buf[:].rearrange("p (c t) -> p c t", c=FC), in_=srcx))
                    P.wait("scalar", lsem, la)
                    if tg >= 1:
                        P.wait("scalar", "pe", hist[tg - 1][1])
                    xqf = xqs[tg % 2]
                    if tg >= 2:
                        P.wait("scalar", "pe", hist[tg - 2][1])
                    acs = P.op("scalar", lambda e, buf=buf, xqf=xqf: e.activation(
                        xqf[:], buf[:], AF.Square), inc="ac")
                    P.wait("tensor", "ac", acs)
                    for c in range(FC):
                        P.op("tensor", lambda e, c=c, xqf=xqf: e.matmul(
                            ps_q[:], ones8[:], xqf[:, c * TG:(c + 1) * TG],
                            start=(c == 0), stop=(c == FC - 1)),
                            inc="pe" if c == FC - 1 else None)
                    pe_stats = P.c("pe")
                    msq = rows[:, 0:TG]
                    sd = rows[:, TG:2 * TG]
                    rs = rows[:, 2 * TG:3 * TG]
                    P.wait("vector", "pe", pe_stats)
                    P.op("vector", lambda e, msq=msq: e.tensor_scalar(
                        msq, ps_q[0:1, :], 1.0 / D, None, ALU.mult))
                    dv1 = P.c("dv")
                    P.wait("scalar", "dv", dv1)
                    ac2 = P.op("scalar", lambda e, msq=msq, sd=sd: e.activation(
                        sd, msq, AF.Sqrt, bias=eps_rms[:]))
                    P.wait("vector", "ac", ac2)
                    P.op("vector", lambda e, sd=sd, rs=rs: e.reciprocal(rs, sd))
                    P.op("vector", lambda e, rs=rs: e.tensor_copy(rs_h[:], rs))
                    dv2 = P.op("vector", lambda e, rs=rs: e.tensor_tensor(
                        rs_l[:], rs, rs_h[:], ALU.subtract))
                    P.wait("tensor", "dv", dv2)
                    pe_r = bcast_psum(ps_r[:], rs_h[:, :], rs_l[:, :])
                    P.wait("vector", "pe", pe_r)
                    if tg >= 1:
                        P.wait("vector", hist[tg - 1][2][0], hist[tg - 1][2][1])
                    for c in range(FC):
                        sl = slice(c * TG, (c + 1) * TG)
                        P.op("vector", lambda e, sl=sl, buf=buf: e.tensor_tensor(
                            t1[:], buf[:, sl], ps_r[:], ALU.mult))
                        dvh = P.op("vector", lambda e, sl=sl, c=c: e.tensor_scalar(
                            hb[:, sl], t1[:], ln1s[:, c:c + 1], None, ALU.mult))
                    dv_h = P.c("dv")
                    dsth = hfm[:, tg * TG:(tg + 1) * TG].rearrange("(c p) t -> p c t", p=128)
                    P.wait("gpsimd", "dv", dv_h)
                    ssem = f"sa{tg % 2}"
                    P.dma("gpsimd", ssem, lambda e, dsth=dsth: e.dma_start(
                        out=dsth, in_=hb[:].rearrange("p (c t) -> p c t", c=FC)))
                    P.wait("tensor", "dv", dv_h)
                    if tg == 0:
                        P.wait("tensor", "lw0", lw_w)
                    evdv = {}
                    for ec in range(FC):
                        pk = ps_k[ec % 2]
                        if ec >= 2:
                            P.wait("tensor", "ac", evdv[ec - 2])
                        elif tg > 0:
                            P.wait("tensor", "ac", P.c("ac"))
                        for c in range(FC):
                            P.op("tensor", lambda e, pk=pk, c=c, ec=ec: e.matmul(
                                pk[:], kws3[:, c, ec * 128:(ec + 1) * 128],
                                hb[:, c * TG:(c + 1) * TG],
                                start=(c == 0), stop=(c == FC - 1)),
                                inc="pe" if c == FC - 1 else None)
                        pe_k = P.c("pe")
                        P.wait("scalar", "pe", pe_k)
                        if ec < 2 and tg >= 1:
                            P.wait("scalar", hist[tg - 1][2][0], hist[tg - 1][2][1])
                        P.op("scalar", lambda e, pk=pk, ec=ec: e.activation(
                            kout[:, ec * TG:(ec + 1) * TG], pk[:], AF.Identity))
                        evdv[ec] = P.c("ac")
                    ac_k = P.c("ac")
                    dstk = kfm[:, tg * TG:(tg + 1) * TG].rearrange("(c p) t -> p c t", p=128)
                    P.wait("gpsimd", "ac", ac_k)
                    sk = P.dma("gpsimd", ssem, lambda e, dstk=dstk: e.dma_start(
                        out=dstk, in_=kout[:].rearrange("p (c t) -> p c t", c=FC)),
                        batch_open=True)
                    hist.append((P.c("ac"), pe_stats, (ssem, P.c(ssem)), dv_h))
                stage_barrier()

        def stage4v():
            with ExitStack() as stk:
                sen = stk.enter_context
                vws = sen(nc.sbuf_tensor("s4v_vws", [128, FC * D], BF16))
                ht = [sen(nc.sbuf_tensor(f"s4v_ht{i}", [128, FC * 128], BF16))
                      for i in range(2)]
                vout = [sen(nc.sbuf_tensor(f"s4v_vout{i}", [128, D], BF16))
                        for i in range(2)]
                ps_v = [sen(nc.psum_tensor(f"s4v_ps_v{i}", [128, TG], F32)) for i in range(2)]
                vws3 = vws[:].rearrange("p (c e) -> p c e", c=FC)
                lw_w = P.dma("gpsimd", "lw0", lambda e: e.dma_start(
                    out=vws[:], in_=wvS[:, :]))
                hist = []
                vguard = {}
                for tc in range(TF // 128):
                    buf = ht[tc % 2]
                    srch = hfm[:, tc * 128:(tc + 1) * 128].rearrange("(c p) t -> p c t", p=128)
                    if tc >= 2:
                        P.wait("sync", "pe", P.c("pe"))
                    lsem = f"la{tc % 2}"
                    la = P.dma("sync", lsem, lambda e, buf=buf, srch=srch: e.dma_start(
                        out=buf[:].rearrange("p (c t) -> p c t", c=FC), in_=srch))
                    P.wait("tensor", lsem, la)
                    if tc == 0:
                        P.wait("tensor", "lw0", lw_w)
                    vo = vout[tc % 2]
                    for egi in range(4):
                        pv = ps_v[egi % 2]
                        gk = tc * 4 + egi
                        if gk >= 2:
                            P.wait("tensor", "ac", vguard[gk - 2])
                        for c in range(FC):
                            P.op("tensor", lambda e, pv=pv, c=c, egi=egi, buf=buf: e.matmul(
                                pv[:], buf[:, c * 128:(c + 1) * 128],
                                vws3[:, c, egi * TG:(egi + 1) * TG],
                                start=(c == 0), stop=(c == FC - 1)),
                                inc="pe" if c == FC - 1 else None)
                        pe_v = P.c("pe")
                        P.wait("scalar", "pe", pe_v)
                        if tc >= 2 and egi == 0:
                            P.wait("scalar", hist[tc - 2][0][0], hist[tc - 2][0][1])
                        P.op("scalar", lambda e, pv=pv, egi=egi, vo=vo: e.activation(
                            vo[:, egi * TG:(egi + 1) * TG], pv[:], AF.Identity))
                        vguard[gk] = P.c("ac")
                    ac_v = P.c("ac")
                    dstv = vtm[:, tc * 128:(tc + 1) * 128, :].rearrange("h t d -> t h d")
                    P.wait("gpsimd", "ac", ac_v)
                    ssem = f"sa{tc % 2}"
                    P.dma("gpsimd", ssem, lambda e, dstv=dstv, vo=vo: e.dma_start(
                        out=dstv, in_=vo[:].rearrange("p (h d) -> p h d", h=H)))
                    hist.append(((ssem, P.c(ssem)),))
                stage_barrier()

        def stage4q():
            with ExitStack() as stk:
                sen = stk.enter_context
                qws = sen(nc.sbuf_tensor("s4q_qws", [128, FC * D], BF16))
                xnt = [sen(nc.sbuf_tensor(f"s4q_xnt{i}", [128, FC * TG], BF16))
                       for i in range(2)]
                hb = sen(nc.sbuf_tensor("s4q_hb", [128, FC * TG], BF16))
                qout = sen(nc.sbuf_tensor("s4q_qout", [128, FC * TG], BF16))
                xqs = [sen(nc.sbuf_tensor(f"s4q_xqs{i}", [128, FC * TG], BF16)) for i in range(2)]
                t1 = sen(nc.sbuf_tensor("s4q_t1", [128, TG], F32))
                rows = sen(nc.sbuf_tensor("s4q_rows", [1, 3 * TG], F32))
                rs_h = sen(nc.sbuf_tensor("s4q_rsh", [1, TG], BF16))
                rs_l = sen(nc.sbuf_tensor("s4q_rsl", [1, TG], BF16))
                ps_q = sen(nc.psum_tensor("s4q_ps_q", [8, TG], F32))
                ps_r = sen(nc.psum_tensor("s4q_ps_r", [128, TG], F32))
                ps_k = [sen(nc.psum_tensor(f"s4q_ps_k{i}", [128, TG], F32)) for i in range(2)]
                qws3 = qws[:].rearrange("p (c e) -> p c e", c=FC)
                lw_w = P.dma("gpsimd", "lw0", lambda e: e.dma_start(
                    out=qws[:], in_=wqS[:, :]))
                hist = []
                for tg in range(TO // TG):
                    buf = xnt[tg % 2]
                    srcx = xnob[:, tg * TG:(tg + 1) * TG].rearrange("(c p) t -> p c t", p=128)
                    if tg >= 2:
                        P.wait("sync", "ac", hist[tg - 2][0])
                        P.wait("sync", "dv", hist[tg - 2][3])
                    lsem = f"la{tg % 2}"
                    la = P.dma("sync", lsem, lambda e, buf=buf, srcx=srcx: e.dma_start(
                        out=buf[:].rearrange("p (c t) -> p c t", c=FC), in_=srcx))
                    P.wait("scalar", lsem, la)
                    if tg >= 1:
                        P.wait("scalar", "pe", hist[tg - 1][1])
                    xqf = xqs[tg % 2]
                    if tg >= 2:
                        P.wait("scalar", "pe", hist[tg - 2][1])
                    acs = P.op("scalar", lambda e, buf=buf, xqf=xqf: e.activation(
                        xqf[:], buf[:], AF.Square), inc="ac")
                    P.wait("tensor", "ac", acs)
                    for c in range(FC):
                        P.op("tensor", lambda e, c=c, xqf=xqf: e.matmul(
                            ps_q[:], ones8[:], xqf[:, c * TG:(c + 1) * TG],
                            start=(c == 0), stop=(c == FC - 1)),
                            inc="pe" if c == FC - 1 else None)
                    pe_stats = P.c("pe")
                    msq = rows[:, 0:TG]
                    sd = rows[:, TG:2 * TG]
                    rs = rows[:, 2 * TG:3 * TG]
                    P.wait("vector", "pe", pe_stats)
                    P.op("vector", lambda e, msq=msq: e.tensor_scalar(
                        msq, ps_q[0:1, :], 1.0 / D, None, ALU.mult))
                    P.wait("scalar", "dv", P.c("dv"))
                    ac2 = P.op("scalar", lambda e, msq=msq, sd=sd: e.activation(
                        sd, msq, AF.Sqrt, bias=eps_rms[:]))
                    P.wait("vector", "ac", ac2)
                    P.op("vector", lambda e, sd=sd, rs=rs: e.reciprocal(rs, sd))
                    P.op("vector", lambda e, rs=rs: e.tensor_copy(rs_h[:], rs))
                    dv2 = P.op("vector", lambda e, rs=rs: e.tensor_tensor(
                        rs_l[:], rs, rs_h[:], ALU.subtract))
                    P.wait("tensor", "dv", dv2)
                    pe_r = bcast_psum(ps_r[:], rs_h[:, :], rs_l[:, :])
                    P.wait("vector", "pe", pe_r)
                    for c in range(FC):
                        sl = slice(c * TG, (c + 1) * TG)
                        P.op("vector", lambda e, sl=sl, buf=buf: e.tensor_tensor(
                            t1[:], buf[:, sl], ps_r[:], ALU.mult))
                        P.op("vector", lambda e, sl=sl, c=c: e.tensor_scalar(
                            hb[:, sl], t1[:], ln1s[:, c:c + 1], None, ALU.mult))
                    dv_h = P.c("dv")
                    P.wait("tensor", "dv", dv_h)
                    if tg == 0:
                        P.wait("tensor", "lw0", lw_w)
                    evdv = {}
                    for ec in range(FC):
                        pk = ps_k[ec % 2]
                        if ec >= 2:
                            P.wait("tensor", "ac", evdv[ec - 2])
                        elif tg > 0:
                            P.wait("tensor", "ac", P.c("ac"))
                        for c in range(FC):
                            P.op("tensor", lambda e, pk=pk, c=c, ec=ec: e.matmul(
                                pk[:], qws3[:, c, ec * 128:(ec + 1) * 128],
                                hb[:, c * TG:(c + 1) * TG],
                                start=(c == 0), stop=(c == FC - 1)),
                                inc="pe" if c == FC - 1 else None)
                        pe_k = P.c("pe")
                        P.wait("scalar", "pe", pe_k)
                        if ec < 2 and tg >= 1:
                            P.wait("scalar", hist[tg - 1][2][0], hist[tg - 1][2][1])
                        P.op("scalar", lambda e, pk=pk, ec=ec: e.activation(
                            qout[:, ec * TG:(ec + 1) * TG], pk[:], AF.Identity))
                        evdv[ec] = P.c("ac")
                    ac_k = P.c("ac")
                    dstq = qfm[:, tg * TG:(tg + 1) * TG].rearrange("(c p) t -> p c t", p=128)
                    P.wait("gpsimd", "ac", ac_k)
                    ssem = f"sa{tg % 2}"
                    P.dma("gpsimd", ssem, lambda e, dstq=dstq: e.dma_start(
                        out=dstq, in_=qout[:].rearrange("p (c t) -> p c t", c=FC)))
                    hist.append((P.c("ac"), pe_stats, (ssem, P.c(ssem)), dv_h))
                stage_barrier()

        if N_STAGES >= 4 and not SKIP_EARLY:
            stage4k()
            stage4v()
            stage4q()

        def stage5():
            with ExitStack() as stk:
                sen = stk.enter_context
                wos = sen(nc.sbuf_tensor("s5_wos", [128, FC * D], BF16))
                maskb = sen(nc.sbuf_tensor("s5_maskb", [128, 32 * TG], BF16))
                aob = sen(nc.sbuf_tensor("s5_aob", [128, H * TG], BF16))
                kb = [sen(nc.sbuf_tensor(f"s5_kb{i}", [128, 32 * 128], BF16))
                      for i in range(2)]
                vb = [sen(nc.sbuf_tensor(f"s5_vb{i}", [128, 32 * 128], BF16))
                      for i in range(2)]
                qb = [sen(nc.sbuf_tensor(f"s5_qb{i}", [128, TG], BF16)) for i in range(2)]
                q2b = sen(nc.sbuf_tensor("s5_q2b", [128, TG], BF16))
                smp = sen(nc.sbuf_tensor("s5_smp", [128, 2 * TG], F32))
                ptp = [sen(nc.sbuf_tensor(f"s5_ptp{i}", [128, 2 * TG], BF16))
                       for i in range(2)]
                zrs = sen(nc.sbuf_tensor("s5_zrs", [128, TG], F32))
                zrow = sen(nc.sbuf_tensor("s5_zrow", [1, TG], F32))
                zh = sen(nc.sbuf_tensor("s5_zh", [1, TG], BF16))
                zl = sen(nc.sbuf_tensor("s5_zl", [1, TG], BF16))
                xot = [sen(nc.sbuf_tensor(f"s5_xot{i}", [128, TG], F32)) for i in range(2)]
                x2t = [sen(nc.sbuf_tensor(f"s5_x2t{i}", [128, TG], F32)) for i in range(2)]
                mkrow = sen(nc.sbuf_tensor("s5_mkrow", [1, FC], F32))
                crow = sen(nc.sbuf_tensor("s5_crow", [1, 8], F32))
                c1b = sen(nc.sbuf_tensor("s5_c1b", [1, 1], BF16))
                biasvec = sen(nc.sbuf_tensor("s5_biasvec", [128, 1], F32))
                ps_Sp = sen(nc.psum_tensor("s5_ps_Sp", [128, 2 * TG], F32))
                ps_O = [sen(nc.psum_tensor(f"s5_ps_O{i}", [128, TG], F32)) for i in range(2)]
                ps_z = [sen(nc.psum_tensor(f"s5_ps_z{i}", [8, TG], F32)) for i in range(2)]
                ps_nc = sen(nc.psum_tensor("s5_ps_nc", [128, TG], F32))
                ps_zr = sen(nc.psum_tensor("s5_ps_zr", [128, TG], F32))

                wos3 = wos[:].rearrange("p (c e) -> p c e", c=FC)
                lw_w = P.dma("gpsimd", "lw0", lambda e: e.dma_start(
                    out=wos[:], in_=woS[:, :]))

                q2p = [q2b, zrs]
                q2x = [sen(nc.sbuf_tensor(f"s5_q2x{i}", [128, TG], BF16)) for i in range(2)]
                c16 = sen(nc.sbuf_tensor("s5_c16", [1, E_EXP], F32))
                c16b = sen(nc.sbuf_tensor("s5_c16b", [1, E_EXP], BF16))
                biasA = sen(nc.sbuf_tensor("s5_biasA", [128, FC], F32))
                sqe = {}
                mme = {}
                rede = {}
                for h in range(H):
                    buf = kb[h % 2]
                    qv = vb[h % 2]
                    srck = kfm[h * 128:(h + 1) * 128, :]
                    srcq = qfm[h * 128:(h + 1) * 128, :]
                    lsem = f"la{h % 2}"
                    if h >= 2:
                        P.wait("sync", "ac", sqe[h - 2])
                    la = P.dma("sync", lsem, lambda e, buf=buf, srck=srck: e.dma_start(
                        out=buf[:], in_=srck))
                    la = P.dma("sync", lsem, lambda e, qv=qv, srcq=srcq: e.dma_start(
                        out=qv[:, 0:TO], in_=srcq), batch_open=True)
                    P.wait("scalar", lsem, la)
                    for g in range(12):
                        tile = (buf[:, g * TG:(g + 1) * TG] if g < 8
                                else qv[:, (g - 8) * TG:(g - 7) * TG])
                        q2t = q2x[g % 2]
                        gp = (h, g - 2) if g >= 2 else (h - 1, 10 + g)
                        if gp in mme:
                            P.wait("scalar", "pe", mme[gp])
                        acs = P.op("scalar", lambda e, tile=tile, q2t=q2t: e.activation(
                            q2t[:], tile, AF.Square))
                        P.wait("tensor", "ac", acs)
                        if gp in rede:
                            P.wait("tensor", "dv", rede[gp])
                        pS = ps_Sp[0:8, (g % 2) * TG:(g % 2) * TG + TG]
                        P.op("tensor", lambda e, pS=pS, q2t=q2t: e.matmul(
                            pS, ones8[:], q2t[:], start=True, stop=True),
                            inc="pe")
                        mme[(h, g)] = P.c("pe")
                        P.wait("vector", "pe", mme[(h, g)])
                        dst = crow[:, 0:1] if g < 8 else crow[:, 1:2]
                        if g in (0, 8):
                            P.op("vector", lambda e, pS=pS, dst=dst: e.tensor_reduce(
                                dst, pS[0:1, :], AX.X, ALU.max))
                        else:
                            P.op("vector", lambda e, pS=pS: e.tensor_reduce(
                                crow[:, 2:3], pS[0:1, :], AX.X, ALU.max))
                            P.op("vector", lambda e, dst=dst: e.tensor_tensor(
                                dst, dst, crow[:, 2:3], ALU.max))
                        rede[(h, g)] = P.c("dv")
                        sqe[h] = P.c("ac")
                    dvm = P.op("vector", lambda e: e.tensor_tensor(
                        crow[:, 3:4], crow[:, 0:1], crow[:, 1:2], ALU.mult))
                    P.wait("scalar", "dv", dvm)
                    P.op("scalar", lambda e, h=h: e.activation(
                        c16[:, h:h + 1], crow[:, 3:4], AF.Sqrt))
                    sqe[h] = P.c("ac")
                P.wait("vector", "ac", P.c("ac"))
                dvc = P.op("vector", lambda e: e.tensor_copy(c16b[:], c16[:]))
                P.wait("tensor", "dv", dvc)
                peb = P.op("tensor", lambda e: e.matmul(
                    ps_nc[:, 0:E_EXP], ones1[:], c16b[:], start=True, stop=True),
                    inc="pe")
                P.wait("vector", "pe", peb)
                dvb_all = P.op("vector", lambda e: e.tensor_scalar(
                    biasA[:], ps_nc[:, 0:E_EXP], -SCALE, None, ALU.mult))
                P.wait("scalar", "dv", dvb_all)
                P.wait("tensor", "dv", dvb_all)
                mk_dv = P.c("dv")

                evac = {}
                for j in range(4):
                    ktp = KTP[j]
                    srcm = mask_tab[j, 0:ktp].rearrange("k p t -> p k t")
                    if j > 0:
                        P.wait("sync", "dv", P.c("dv"))
                    lam = P.dma("sync", "lc", lambda e, srcm=srcm, ktp=ktp: e.dma_start(
                        out=maskb[:].rearrange("p (k t) -> p k t", k=32)[:, 0:ktp, :],
                        in_=srcm))
                    h_fin = {}
                    h_dva = {}
                    h_zread = {}
                    for h in range(H):
                        kbuf = kb[h % 2]
                        vbuf = vb[h % 2]
                        qbuf = qb[h % 2]
                        lsem = f"la{h % 2}"
                        if h >= 2:
                            P.wait("sync", "pe", h_fin[h - 2])
                        else:
                            P.wait("sync", "pe", P.c("pe"))
                            P.wait("sync", "ac", P.c("ac"))
                        la = P.dma("sync", lsem, lambda e, kbuf=kbuf, h=h, ktp=ktp:
                                   e.dma_start(out=kbuf[:, 0:ktp * 128],
                                               in_=kfm[h * 128:(h + 1) * 128, 0:ktp * 128]))
                        srcv = vtm[h, 0:ktp * 128, :].rearrange("(kt p) d -> p kt d", p=128)
                        P.dma("sync", lsem, lambda e, vbuf=vbuf, srcv=srcv, ktp=ktp:
                              e.dma_start(out=vbuf[:].rearrange(
                                  "p (kt d) -> p kt d", kt=32)[:, 0:ktp, :], in_=srcv),
                              batch_open=True)
                        srcq = qfm[h * 128:(h + 1) * 128, j * TG:(j + 1) * TG]
                        la = P.dma("sync", lsem, lambda e, qbuf=qbuf, srcq=srcq:
                                   e.dma_start(out=qbuf[:], in_=srcq), batch_open=True)
                        P.wait("scalar", lsem, la)
                        P.wait("tensor", lsem, la)
                        if j == 0 and h == 0:
                            P.wait("tensor", "lc", lam)
                            P.wait("vector", "lc", lam)
                        elif h == 0:
                            P.wait("vector", "lc", lam)
                        guards = {}
                        if h >= 2:
                            P.wait("tensor", "dv", max(h_dva[h - 2], h_zread[h - 2]))
                        elif j > 0:
                            P.wait("tensor", "dv", P.c("dv"))
                        npair = ktp // 2
                        for q in range(npair):
                            masked = (2 * q) >= ktp - 8
                            ptb = ptp[q % 2]
                            if q >= 1:
                                g0 = guards[q - 1]
                                if g0[0] is not None:
                                    P.wait("tensor", "dv", g0[0])
                                else:
                                    P.wait("tensor", "ac", g0[1])
                            for half in range(2):
                                kt = 2 * q + half
                                P.op("tensor", lambda e, kbuf=kbuf, qbuf=qbuf, kt=kt,
                                     half=half: e.matmul(
                                         ps_Sp[:, half * TG:(half + 1) * TG],
                                         kbuf[:, kt * 128:(kt + 1) * 128],
                                         qbuf[:], start=True, stop=True),
                                     inc="pe" if half == 1 else None)
                            peS = P.c("pe")
                            if q >= 1:
                                P.wait("tensor", "ac", guards[q - 1][1])
                                for half in range(2):
                                    kt = 2 * (q - 1) + half
                                    pb = ptp[(q - 1) % 2]
                                    P.op("tensor", lambda e, vbuf=vbuf, kt=kt, pb=pb,
                                         half=half, pO=ps_O[h % 2], ktp=ktp: e.matmul(
                                             pO[:], vbuf[:, kt * 128:(kt + 1) * 128],
                                             pb[:, half * TG:(half + 1) * TG],
                                             start=(kt == 0), stop=False))
                                    P.op("tensor", lambda e, kt=kt, pb=pb, half=half,
                                         pz=ps_z[h % 2], ktp=ktp: e.matmul(
                                             pz[:], ones8[:],
                                             pb[:, half * TG:(half + 1) * TG],
                                             start=(kt == 0), stop=False),
                                         inc="pe" if (kt == 2 * q - 1) else None)
                            if masked:
                                P.wait("vector", "pe", peS)
                                if q >= 1 and guards[q - 1][1] is not None:
                                    P.wait("vector", "ac", guards[q - 1][1])
                                dvS = P.op("vector", lambda e, q=q: e.tensor_tensor(
                                    smp[:], ps_Sp[:],
                                    maskb[:, 2 * q * TG:2 * q * TG + 2 * TG], ALU.add))
                                P.wait("scalar", "dv", dvS)
                                if q >= 2:
                                    P.wait("scalar", "pe", P.c("pe"))
                                acE = P.op("scalar", lambda e, ptb=ptb, h=h: e.activation(
                                    ptb[:], smp[:], AF.Exp, bias=biasA[:, h:h + 1],
                                    scale=SCALE))
                            else:
                                dvS = None
                                P.wait("scalar", "pe", peS)
                                acE = P.op("scalar", lambda e, ptb=ptb, h=h: e.activation(
                                    ptb[:], ps_Sp[:], AF.Exp, bias=biasA[:, h:h + 1],
                                    scale=SCALE))
                            guards[q] = (dvS, acE)
                        P.wait("tensor", "ac", guards[npair - 1][1])
                        for half in range(2):
                            kt = 2 * (npair - 1) + half
                            pb = ptp[(npair - 1) % 2]
                            P.op("tensor", lambda e, vbuf=vbuf, kt=kt, pb=pb, half=half,
                                 pO=ps_O[h % 2], ktp=ktp: e.matmul(
                                     pO[:], vbuf[:, kt * 128:(kt + 1) * 128],
                                     pb[:, half * TG:(half + 1) * TG],
                                     start=(kt == 0), stop=(kt == ktp - 1)))
                            P.op("tensor", lambda e, kt=kt, pb=pb, half=half,
                                 pz=ps_z[h % 2], ktp=ktp: e.matmul(
                                     pz[:], ones8[:], pb[:, half * TG:(half + 1) * TG],
                                     start=(kt == 0), stop=(kt == ktp - 1)),
                                 inc="pe" if kt == ktp - 1 else None)
                        pe_fin = P.c("pe")
                        h_fin[h] = pe_fin
                        P.wait("vector", "pe", pe_fin)
                        zr0 = P.op("vector", lambda e, pz=ps_z[h % 2]: e.tensor_copy(
                            zrow[:], pz[0:1, :]))
                        h_zread[h] = zr0
                        P.op("vector", lambda e: e.reciprocal(zrow[:], zrow[:]))
                        P.op("vector", lambda e: e.tensor_copy(zh[:], zrow[:]))
                        dvz = P.op("vector", lambda e: e.tensor_tensor(
                            zl[:], zrow[:], zh[:], ALU.subtract))
                        P.wait("tensor", "dv", dvz)
                        pe_zr = bcast_psum(ps_zr[:], zh[:, :], zl[:, :])
                        P.wait("scalar", "pe", pe_zr)
                        ac_zr = P.op("scalar", lambda e: e.activation(
                            zrs[:], ps_zr[:], AF.Identity))
                        P.wait("vector", "ac", ac_zr)
                        dva = P.op("vector", lambda e, h=h, pO=ps_O[h % 2]: e.tensor_tensor(
                            aob[:, h * TG:(h + 1) * TG], pO[:], zrs[:], ALU.mult))
                        h_dva[h] = dva
                    dv_ao = max(h_dva.values())
                    P.wait("tensor", "dv", dv_ao)
                    if j == 0:
                        P.wait("tensor", "lw0", lw_w)
                    for oc in range(FC):
                        gi = j * FC + oc
                        pw = ps_Sp[:, (oc % 2) * TG:(oc % 2) * TG + TG]
                        xob = xot[oc % 2]
                        lsem = f"la{oc % 2}"
                        srcx = xnof[oc * 128:(oc + 1) * 128, j * TG:(j + 1) * TG]
                        if gi >= 2:
                            P.wait("sync", "dv", evac[gi - 2][0])
                        la = P.dma("sync", lsem, lambda e, xob=xob, srcx=srcx:
                                   e.dma_start(out=xob[:], in_=srcx))
                        if gi >= 2:
                            P.wait("tensor", "dv", evac[gi - 2][0])
                        for ec in range(FC):
                            P.op("tensor", lambda e, pw=pw, ec=ec, oc=oc: e.matmul(
                                pw, wos3[:, ec, oc * 128:(oc + 1) * 128],
                                aob[:, ec * TG:(ec + 1) * TG],
                                start=(ec == 0), stop=(ec == FC - 1)),
                                inc="pe" if ec == FC - 1 else None)
                        pe_w = P.c("pe")
                        P.wait("vector", "pe", pe_w)
                        P.wait("vector", lsem, la)
                        if gi >= 2:
                            P.wait("vector", evac[gi - 2][1][0], evac[gi - 2][1][1])
                        x2b = x2t[oc % 2]
                        dvw = P.op("vector", lambda e, pw=pw, xob=xob, x2b=x2b:
                                   e.tensor_tensor(x2b[:], pw, xob[:], ALU.add))
                        dstx = x2d[oc * 128:(oc + 1) * 128, j * TG:(j + 1) * TG]
                        P.wait("gpsimd", "dv", dvw)
                        ssem = f"sa{oc % 2}"
                        sv = P.dma("gpsimd", ssem, lambda e, dstx=dstx, x2b=x2b:
                                   e.dma_start(out=dstx, in_=x2b[:]))
                        evac[gi] = (dvw, (ssem, sv))
                stage_barrier()

        if N_STAGES >= 5 and not SKIP_EARLY:
            stage5()

        def stage7(p):
            TP = 1024
            with ExitStack() as stk:
                sen = stk.enter_context
                tfm = sen(nc.sbuf_tensor(f"s7_tfm{p}", [128, FC * TP], BF16))
                accb = sen(nc.sbuf_tensor(f"s7_acc{p}", [128, FC * TP], F32))
                edres = sen(nc.sbuf_tensor(f"s7_ed{p}", [128, IC * D], BF16))
                egb = [sen(nc.sbuf_tensor(f"s7_egb{p}_{i}", [128, FC * 128], BF16))
                       for i in range(2)]
                eub = [sen(nc.sbuf_tensor(f"s7_eub{p}_{i}", [128, FC * 128], BF16))
                       for i in range(2)]
                hh = [sen(nc.sbuf_tensor(f"s7_hh{p}_{i}", [128, IC * TG], BF16))
                      for i in range(2)]
                sgt = sen(nc.sbuf_tensor(f"s7_sgt{p}", [128, TG], F32))
                h1t = sen(nc.sbuf_tensor(f"s7_h1t{p}", [128, TG], F32))
                cwh_r = sen(nc.sbuf_tensor(f"s7_cwhr{p}", [1, TP], BF16))
                cwl_r = sen(nc.sbuf_tensor(f"s7_cwlr{p}", [1, TP], BF16))

                col0 = p * TP

                with ExitStack() as stkA:
                    ps_stats = stkA.enter_context(
                        nc.psum_tensor(f"s7_ps_st{p}", [8, TG], F32))
                    ps_r = stkA.enter_context(
                        nc.psum_tensor(f"s7_ps_r{p}", [128, TG], F32))
                    senA = stkA.enter_context
                    x2c = [senA(nc.sbuf_tensor(f"s7_x2c{p}_{i}", [128, TG], F32))
                           for i in range(2)]
                    a1c = [senA(nc.sbuf_tensor(f"s7_a1c{p}_{i}", [128, TG], F32))
                           for i in range(2)]
                    xq = senA(nc.sbuf_tensor(f"s7_xq{p}", [128, 2 * TG], BF16))
                    rows = senA(nc.sbuf_tensor(f"s7_rows{p}", [1, 2 * TG], F32))
                    rs_h = senA(nc.sbuf_tensor(f"s7_rsh{p}", [1, TG], BF16))
                    rs_l = senA(nc.sbuf_tensor(f"s7_rsl{p}", [1, TG], BF16))
                    for tgo in range(2):
                        co = col0 + tgo * TG
                        P.wait("sync", "ac", P.c("ac"))
                        P.wait("sync", "dv", P.c("dv"))
                        mmpe = {}
                        for c in range(FC):
                            xc = x2c[c % 2]
                            lsem = f"la{c % 2}"
                            if c >= 2:
                                P.wait("sync", "ac", mmpe[c - 2][1])
                            la = P.dma("sync", lsem, lambda e, xc=xc, c=c, co=co:
                                       e.dma_start(out=xc[:], in_=x2d[
                                           c * 128:(c + 1) * 128, co:co + TG]))
                            xq2 = xq[:, (c % 2) * TG:(c % 2) * TG + TG]
                            P.wait("scalar", lsem, la)
                            if c >= 2:
                                P.wait("scalar", "pe", mmpe[c - 2][0])
                            acs = P.op("scalar", lambda e, xc=xc, xq2=xq2: e.activation(
                                xq2, xc[:], AF.Square))
                            P.wait("tensor", "ac", acs)
                            P.op("tensor", lambda e, c=c, xq2=xq2: e.matmul(
                                ps_stats[:], ones8[:], xq2,
                                start=(c == 0), stop=(c == FC - 1)), inc="pe")
                            mmpe[c] = (P.c("pe"), acs)
                        pe_stats = P.c("pe")
                        msq = rows[:, 0:TG]
                        sd = rows[:, 0:TG]
                        rs = rows[:, TG:2 * TG]
                        P.wait("vector", "pe", pe_stats)
                        P.op("vector", lambda e, msq=msq: e.tensor_scalar(
                            msq, ps_stats[0:1, :], 1.0 / D, None, ALU.mult))
                        P.wait("scalar", "dv", P.c("dv"))
                        ac2 = P.op("scalar", lambda e, msq=msq, sd=sd: e.activation(
                            sd, msq, AF.Sqrt, bias=eps_rms[:]))
                        P.wait("vector", "ac", ac2)
                        P.op("vector", lambda e, sd=sd, rs=rs: e.reciprocal(rs, sd))
                        P.op("vector", lambda e, rs=rs: e.tensor_copy(rs_h[:], rs))
                        dv2 = P.op("vector", lambda e, rs=rs: e.tensor_tensor(
                            rs_l[:], rs, rs_h[:], ALU.subtract))
                        P.wait("tensor", "dv", dv2)
                        pe_r = bcast_psum(ps_r[:], rs_h[:, :], rs_l[:, :])
                        P.wait("vector", "pe", pe_r)
                        P.wait("sync", "ac", P.c("ac"))
                        P.wait("sync", "dv", P.c("dv"))
                        prev = {}
                        for c in range(FC):
                            xc = x2c[c % 2]
                            ac1t = a1c[c % 2]
                            lsem = f"la{c % 2}"
                            if c >= 2:
                                P.wait("sync", "dv", prev[c - 2])
                            la = P.dma("sync", lsem, lambda e, xc=xc, c=c, co=co:
                                       e.dma_start(out=xc[:], in_=x2d[
                                           c * 128:(c + 1) * 128, co:co + TG]))
                            la = P.dma("sync", lsem, lambda e, ac1t=ac1t, c=c, co=co:
                                       e.dma_start(out=ac1t[:], in_=acc1[
                                           c * 128:(c + 1) * 128, co:co + TG]),
                                       batch_open=True)
                            tsl = slice(c * TP + tgo * TG, c * TP + tgo * TG + TG)
                            P.wait("vector", lsem, la)
                            P.op("vector", lambda e, xc=xc, tsl=tsl: e.tensor_tensor(
                                h1t[:], xc[:], ps_r[:], ALU.mult))
                            P.op("vector", lambda e, tsl=tsl, c=c: e.tensor_scalar(
                                tfm[:, tsl], h1t[:], ln2s[:, c:c + 1], None, ALU.mult))
                            dvp = P.op("vector", lambda e, xc=xc, ac1t=ac1t, tsl=tsl:
                                       e.tensor_tensor(accb[:, tsl], xc[:], ac1t[:],
                                                       ALU.add))
                            prev[c] = dvp
                    stage_barrier()

                with ExitStack() as stkB:
                    ps_g = stkB.enter_context(
                        nc.psum_tensor(f"s7_ps_g{p}", [128, E_EXP], F32))
                    ps_cwT = stkB.enter_context(
                        nc.psum_tensor(f"s7_ps_cwT{p}", [16, 128], F32))
                    senB = stkB.enter_context
                    gws = senB(nc.sbuf_tensor(f"s7_gws{p}", [128, FC * E_EXP], BF16))
                    lg = senB(nc.sbuf_tensor(f"s7_lg{p}", [128, E_EXP], F32))
                    lgw = senB(nc.sbuf_tensor(f"s7_lgw{p}", [128, E_EXP], F32))
                    sel = senB(nc.sbuf_tensor(f"s7_sel{p}", [128, E_EXP], F32))
                    ex = senB(nc.sbuf_tensor(f"s7_ex{p}", [128, E_EXP], F32))
                    mcol = senB(nc.sbuf_tensor(f"s7_m{p}", [128, 8], F32))
                    negm0 = senB(nc.sbuf_tensor(f"s7_negm0{p}", [128, 1], F32))
                    ssum = senB(nc.sbuf_tensor(f"s7_ssum{p}", [128, 2], F32))
                    cwh = senB(nc.sbuf_tensor(f"s7_cwh{p}", [128, E_EXP], BF16))
                    cwl = senB(nc.sbuf_tensor(f"s7_cwl{p}", [128, E_EXP], BF16))
                    tcwh = senB(nc.sbuf_tensor(f"s7_tcwh{p}", [16, 128], BF16))
                    tcwl = senB(nc.sbuf_tensor(f"s7_tcwl{p}", [16, 128], BF16))
                    lag = P.dma("gpsimd", "lw0", lambda e: e.dma_start(
                        out=gws[:], in_=gwS[:, :]))
                    P.wait("tensor", "lw0", lag)
                    for ti in range(8):
                        if ti > 0:
                            P.wait("tensor", "dv", P.c("dv"))
                        for c in range(FC):
                            P.op("tensor", lambda e, c=c, ti=ti: e.matmul(
                                ps_g[:], tfm[:, c * TP + ti * 128:c * TP + ti * 128 + 128],
                                gws[:].rearrange("p (c e) -> p c e", c=FC)[:, c, :],
                                start=(c == 0), stop=(c == FC - 1)),
                                inc="pe" if c == FC - 1 else None)
                        pe_g = P.c("pe")
                        P.wait("vector", "pe", pe_g)
                        P.op("vector", lambda e: e.tensor_copy(lg[:], ps_g[:]))
                        P.op("vector", lambda e: e.tensor_copy(lgw[:], lg[:]))
                        for r_ in range(TOPK):
                            P.op("vector", lambda e, r_=r_: e.tensor_reduce(
                                mcol[:, r_:r_ + 1], lgw[:], AX.X, ALU.max))
                            P.op("vector", lambda e, r_=r_: e.tensor_scalar(
                                sel[:], lgw[:], mcol[:, r_:r_ + 1], None, ALU.is_equal))
                            P.op("vector", lambda e: e.scalar_tensor_tensor(
                                lgw[:], sel[:], -1e30, lgw[:], ALU.mult, ALU.add))
                        P.op("vector", lambda e: e.tensor_scalar(
                            negm0[:], mcol[:, 0:1], -1.0, None, ALU.mult))
                        dvt = P.op("vector", lambda e: e.tensor_scalar(
                            sel[:], lg[:], mcol[:, TOPK - 1:TOPK], None, ALU.is_ge))
                        P.wait("scalar", "dv", dvt)
                        ace = P.op("scalar", lambda e: e.activation(
                            ex[:], lg[:], AF.Exp, bias=negm0[:], scale=1.0))
                        P.wait("vector", "ac", ace)
                        P.op("vector", lambda e: e.tensor_tensor(
                            ex[:], ex[:], sel[:], ALU.mult))
                        P.op("vector", lambda e: e.tensor_reduce(
                            ssum[:, 0:1], ex[:], AX.X, ALU.add))
                        P.op("vector", lambda e: e.reciprocal(
                            ssum[:, 1:2], ssum[:, 0:1]))
                        P.op("vector", lambda e: e.tensor_scalar(
                            ex[:], ex[:], ssum[:, 1:2], None, ALU.mult))
                        P.op("vector", lambda e: e.tensor_copy(cwh[:], ex[:]))
                        dvw = P.op("vector", lambda e: e.tensor_tensor(
                            cwl[:], ex[:], cwh[:], ALU.subtract))
                        P.wait("tensor", "dv", dvw)
                        peh = P.op("tensor", lambda e: e.matmul(
                            ps_cwT[:], cwh[:], idb[:], start=True, stop=True), inc="pe")
                        P.wait("vector", "pe", peh)
                        if ti > 0:
                            P.wait("vector", f"sa{(ti - 1) % 2}", P.c(f"sa{(ti - 1) % 2}"))
                        dvh = P.op("vector", lambda e: e.tensor_copy(tcwh[:], ps_cwT[:]))
                        P.wait("tensor", "dv", dvh)
                        pel = P.op("tensor", lambda e: e.matmul(
                            ps_cwT[:], cwl[:], idb[:], start=True, stop=True), inc="pe")
                        P.wait("vector", "pe", pel)
                        dvc = P.op("vector", lambda e: e.tensor_copy(tcwl[:], ps_cwT[:]))
                        P.wait("gpsimd", "dv", dvc)
                        P.dma("gpsimd", f"sa{ti % 2}", lambda e, ti=ti: e.dma_start(
                            out=cwTdh[:, ti * 128:(ti + 1) * 128], in_=tcwh[:]))
                        P.dma("gpsimd", f"sa{ti % 2}", lambda e, ti=ti: e.dma_start(
                            out=cwTdl[:, ti * 128:(ti + 1) * 128], in_=tcwl[:]),
                            batch_open=True)
                    stage_barrier()

                with ExitStack() as stkC:
                    ps_G = [stkC.enter_context(
                        nc.psum_tensor(f"s7_ps_G{p}_{i}", [128, TG], F32)) for i in range(2)]
                    ps_U = [stkC.enter_context(
                        nc.psum_tensor(f"s7_ps_U{p}_{i}", [128, TG], F32)) for i in range(2)]
                    ps_D = [stkC.enter_context(
                        nc.psum_tensor(f"s7_ps_D{p}_{i}", [128, TG], F32)) for i in range(2)]
                    ps_cwb = [stkC.enter_context(
                        nc.psum_tensor(f"s7_ps_cwb{p}_{i}", [128, TG], F32))
                        for i in range(2)]

                    def expert(e_idx, shared_half):
                        nic = IC
                        if shared_half is None:
                            ed_src = edT[e_idx].rearrange("(ic q) f -> q ic f", q=128)
                            sic0 = 0
                        else:
                            r0 = shared_half * IC * 128
                            ed_src = sdT[r0:r0 + IC * 128, :].rearrange(
                                "(ic q) f -> q ic f", q=128)
                            up_g = None
                            sic0 = shared_half * IC
                        P.wait("gpsimd", "pe", P.c("pe"))
                        lwe = P.dma("gpsimd", "lw1", lambda e, ed_src=ed_src: e.dma_start(
                            out=edres[:].rearrange("p (ic f) -> p ic f", ic=IC),
                            in_=ed_src))
                        if shared_half is None:
                            P.wait("sync", "pe", P.c("pe"))
                            lcw = P.dma("sync", "lc", lambda e, e_idx=e_idx: e.dma_start(
                                out=cwh_r[:], in_=cwTdh[e_idx:e_idx + 1, :]))
                            lcw = P.dma("sync", "lc", lambda e, e_idx=e_idx: e.dma_start(
                                out=cwl_r[:], in_=cwTdl[e_idx:e_idx + 1, :]),
                                batch_open=True)
                            P.wait("tensor", "lc", lcw)
                        def slab_load(ic):
                            gb = egb[ic % 2]
                            ub = eub[ic % 2]
                            if shared_half is None:
                                sg_src = egS[e_idx, ic]
                                su_src = euS[e_idx, ic]
                            else:
                                sg_src = sgS[sic0 + ic]
                                su_src = suS[sic0 + ic]
                            P.dma("gpsimd", "lw0", lambda e, gb=gb, sg_src=sg_src:
                                  e.dma_start(out=gb[:], in_=sg_src[:, :]))
                            return P.dma("gpsimd", "lw0",
                                         lambda e, ub=ub, su_src=su_src:
                                         e.dma_start(out=ub[:], in_=su_src[:, :]),
                                         batch_open=True)
                        slab_pe = {}
                        chain_guard = {}
                        lws_of = {0: slab_load(0)}
                        for ic in range(nic):
                            gb = egb[ic % 2]
                            ub = eub[ic % 2]
                            lws = lws_of[ic]
                            if ic + 1 < nic:
                                if ic >= 1:
                                    P.wait("gpsimd", "pe", slab_pe[ic - 1])
                                lws_of[ic + 1] = slab_load(ic + 1)
                            for tgo in range(2):
                                pG = ps_G[tgo]
                                pU = ps_U[tgo]
                                if ic > 0:
                                    P.wait("tensor", "dv", chain_guard[(ic - 1, tgo)])
                                P.wait("tensor", "lw0", lws)
                                for c in range(FC):
                                    P.op("tensor", lambda e, pG=pG, gb=gb, c=c, tgo=tgo:
                                         e.matmul(pG[:],
                                                  gb[:, c * 128:(c + 1) * 128],
                                                  tfm[:, c * TP + tgo * TG:
                                                      c * TP + tgo * TG + TG],
                                                  start=(c == 0), stop=(c == FC - 1)))
                                    P.op("tensor", lambda e, pU=pU, ub=ub, c=c, tgo=tgo:
                                         e.matmul(pU[:],
                                                  ub[:, c * 128:(c + 1) * 128],
                                                  tfm[:, c * TP + tgo * TG:
                                                      c * TP + tgo * TG + TG],
                                                  start=(c == 0), stop=(c == FC - 1)),
                                         inc="pe" if c == FC - 1 else None)
                                pe_gu = P.c("pe")
                                if shared_half is None and ic == 0:
                                    csl = slice(tgo * TG, tgo * TG + TG)
                                    P.wait("tensor", "dv", P.c("dv"))
                                    P.op("tensor", lambda e, csl=csl, tgo=tgo: e.matmul(
                                        ps_cwb[tgo][:], ones1[:], cwh_r[0:1, csl],
                                        start=True, stop=False))
                                    pcw = P.op("tensor", lambda e, csl=csl, tgo=tgo:
                                               e.matmul(ps_cwb[tgo][:], ones1[:],
                                                        cwl_r[0:1, csl],
                                                        start=False, stop=True), inc="pe")
                                    pe_gu = pcw
                                P.wait("scalar", "pe", pe_gu)
                                prev_key = (ic, tgo - 1) if tgo else (ic - 1, 1)
                                if prev_key in chain_guard:
                                    P.wait("scalar", "dv", chain_guard[prev_key])
                                else:
                                    P.wait("scalar", "dv", P.c("dv"))
                                acs = P.op("scalar", lambda e, pG=pG: e.activation(
                                    sgt[:], pG[:], AF.Silu))
                                P.wait("vector", "ac", acs)
                                P.op("vector", lambda e, pU=pU: e.tensor_tensor(
                                    h1t[:], sgt[:], pU[:], ALU.mult))
                                hsl = slice(ic * TG, (ic + 1) * TG)
                                if shared_half is None:
                                    P.op("vector", lambda e, hsl=hsl, tgo=tgo:
                                         e.tensor_tensor(hh[tgo][:, hsl], h1t[:],
                                                         ps_cwb[tgo][:], ALU.mult))
                                else:
                                    P.op("vector", lambda e, hsl=hsl, tgo=tgo:
                                         e.tensor_copy(hh[tgo][:, hsl], h1t[:]))
                                chain_guard[(ic, tgo)] = P.c("dv")
                                slab_pe[ic] = pe_gu
                        dv_hh = P.c("dv")
                        P.wait("tensor", "dv", dv_hh)
                        P.wait("tensor", "lw1", lwe)
                        dctx = {}
                        for tgo in range(2):
                            for fc in range(FC):
                                k = tgo * FC + fc
                                pD = ps_D[k % 2]
                                if k >= 2:
                                    P.wait("tensor", "dv", dctx[k - 2])
                                for ic in range(nic):
                                    P.op("tensor", lambda e, pD=pD, ic=ic, fc=fc, tgo=tgo:
                                         e.matmul(pD[:],
                                                  edres[:, ic * D + fc * 128:
                                                        ic * D + fc * 128 + 128],
                                                  hh[tgo][:, ic * TG:(ic + 1) * TG],
                                                  start=(ic == 0), stop=(ic == nic - 1)),
                                         inc="pe" if ic == nic - 1 else None)
                                pe_d = P.c("pe")
                                P.wait("vector", "pe", pe_d)
                                asl = slice(fc * TP + tgo * TG, fc * TP + tgo * TG + TG)
                                dva = P.op("vector", lambda e, pD=pD, asl=asl:
                                           e.tensor_tensor(accb[:, asl], pD[:],
                                                           accb[:, asl], ALU.add))
                                dctx[k] = dva

                    for e_idx in range(E_EXP):
                        expert(e_idx, None)
                    expert(None, 0)
                    expert(None, 1)
                    dv_fin = P.c("dv")
                    P.wait("gpsimd", "dv", dv_fin)
                    for tgo in range(2):
                        srcv = accb[:].rearrange("p (c u) -> p c u", u=TP)[
                            :, :, tgo * TG:(tgo + 1) * TG]
                        dsto = out[:, col0 + tgo * TG:col0 + tgo * TG + TG].rearrange(
                            "(c p) t -> p c t", p=128)
                        P.dma("gpsimd", f"sa{tgo}", lambda e, srcv=srcv, dsto=dsto:
                              e.dma_start(out=dsto, in_=srcv))
                    stage_barrier()

        if N_STAGES >= 7:
            stage7(0)
            stage7(1)

        for s in ("sa0", "sa1", "sy", "lw0", "lw1", "la0", "la1", "lc",
                  "pe", "dv", "ac", "po"):
            P.wait("sync", s, P.c(s))

        with nc.Block() as block:
            P.replay(block, sems)

    print("instruction counts:", P.n_ops, "total", sum(P.n_ops.values()))
    return nc



OWN_STRIPES = [[0, 3, 4, 7], [1, 2, 5, 6]]


def make_masks(parity):
    tab = np.zeros((4, 32, 128, TG), np.float32)
    stripes = OWN_STRIPES[parity]
    for j, s in enumerate(stripes):
        for kt in range(KTP[j]):
            kg = kt * 128 + np.arange(128)[:, None]
            qg = s * TG + np.arange(TG)[None, :]
            tab[j, kt] = np.where(kg <= qg, 0.0, NEG)
    return tab.astype(ml_dtypes.bfloat16)


def prep_inputs(inputs):
    g = lambda k: np.asarray(inputs[k])
    x = g("L0_attn_hidden_states")
    xf = [np.ascontiguousarray(x[b].T) for b in range(4)]
    def wswz(a):
        n = a.shape[1]
        return np.ascontiguousarray(
            a.reshape(FC, 128, n).transpose(1, 0, 2).reshape(128, FC * n))

    def upswz(a):
        E0 = a.shape[0]
        return np.ascontiguousarray(
            a.reshape(E0, FC, 128, IC, 128).transpose(0, 3, 2, 1, 4).reshape(
                E0, IC, 128, FC * 128))

    def shswz(a):
        return np.ascontiguousarray(
            a.reshape(FC, 128, SIC, 128).transpose(2, 1, 0, 3).reshape(
                SIC, 128, FC * 128))

    shared = {
        "qwS": wswz(np.ascontiguousarray(g("L0_attn_q_w").T)),
        "kwS": wswz(np.ascontiguousarray(g("L0_attn_k_w").T)),
        "vwS": wswz(np.ascontiguousarray(g("L0_attn_v_w").T)),
        "owS": wswz(np.ascontiguousarray(g("L0_attn_o_w").T)),
        "wqS": wswz(g("L0_ffn_wq")),
        "wkS": wswz(g("L0_ffn_wk")),
        "wvS": wswz(g("L0_ffn_wv")),
        "woS": wswz(g("L0_ffn_wo")),
        "gwS": wswz(np.ascontiguousarray(g("L0_ffn_gate_w").T)),
        "egS": upswz(g("L0_ffn_eg_w").transpose(0, 2, 1)),
        "euS": upswz(g("L0_ffn_eu_w").transpose(0, 2, 1)),
        "edT": np.ascontiguousarray(g("L0_ffn_ed_w").transpose(0, 2, 1)),
        "sgS": shswz(g("L0_ffn_sg_w").T),
        "suS": shswz(g("L0_ffn_su_w").T),
        "sdT": np.ascontiguousarray(g("L0_ffn_sd_w").T),
        "qb2": np.ascontiguousarray(g("L0_attn_q_b").reshape(FC, 128).T),
        "ob2": np.ascontiguousarray(g("L0_attn_o_b").reshape(FC, 128).T),
        "kbr": np.ascontiguousarray(g("L0_attn_k_b").reshape(1, D)),
        "vbr": np.ascontiguousarray(g("L0_attn_v_b").reshape(1, D)),
        "ln1c": np.ascontiguousarray(g("L0_ffn_ln1_w").reshape(FC, 128).T),
        "ln2c": np.ascontiguousarray(g("L0_ffn_ln2_w").reshape(FC, 128).T),
        "idenb": np.eye(128, dtype=ml_dtypes.bfloat16),
    }
    masks = [make_masks(0), make_masks(1)]
    in_maps = []
    for core in range(8):
        b, p = core // 2, core % 2
        stripes = OWN_STRIPES[p]
        xo = np.concatenate([xf[b][:, s * TG:(s + 1) * TG] for s in stripes], axis=1)
        m = dict(shared)
        m["x_full"] = xf[b]
        m["x_own"] = np.ascontiguousarray(xo)
        m["mask_tab"] = masks[p]
        in_maps.append(m)
    return in_maps


def gather_output(results):
    out = np.zeros((4, TF, D), np.float32)
    for core in range(8):
        b, p = core // 2, core % 2
        o = np.asarray(results[core]["out"], np.float32)
        for j, s in enumerate(OWN_STRIPES[p]):
            out[b, s * TG:(s + 1) * TG, :] = o[:, j * TG:(j + 1) * TG].T
    return out


_CACHED = {}


def kernel(**inputs):
    if "nc" not in _CACHED:
        _CACHED["nc"] = build_program()
    nc = _CACHED["nc"]
    in_maps = prep_inputs(inputs)
    res = run_bass_kernel_spmd(nc, in_maps, core_ids=list(range(8)))
    _CACHED["res"] = res
    return gather_output(res.results)
